# version 40
# baseline (speedup 1.0000x reference)
import contextlib
import numpy as np

import bass_rust
import concourse.bass as bass
import concourse.mybir as mybir
from concourse.bass_utils import run_bass_kernel_spmd

F32 = mybir.dt.float32
BF16 = mybir.dt.bfloat16
I32 = mybir.dt.int32
AF = mybir.ActivationFunctionType
ALU = mybir.AluOpType
AXX = mybir.AxisListType.X
AXC = mybir.AxisListType.C
RO_ABSMAX = bass_rust.ReduceOp.absmax

T = 2048
D = 1024
DC = 8
II = 2816
CI = 22
TBW = 512
NTB = 4
NCORES = 8
LN_EPS = 1e-5
R127 = float(np.float32(1.0) / np.float32(127.0))
CLIP_LO = -128.49
CLIP_HI = 127.49
RND_MAGIC = 12582912.0
ARENA_W = 52400
PE_FULL_TAGS = frozenset()


class Eng:
    def __init__(self, name, blockname, sem, dma_sems=()):
        self.name = name
        self.blockname = blockname
        self.sem = sem
        self.n = 0
        self.seen = {}
        self.items = []
        self.dch = [[s, 0] for s in dma_sems]
        self.rr = 0

    def _prep(self, deps):
        ws = []
        for d in deps:
            if d is None:
                continue
            sem, val = d
            k = id(sem)
            if self.seen.get(k, 0) >= val:
                continue
            self.seen[k] = val
            ws.append((sem, val))
        return ws

    def op(self, deps, fn, inc=1):
        ws = self._prep(deps)
        tok = None
        if inc:
            self.n += inc
            tok = (self.sem, self.n)
        sem = self.sem

        def item(e, ws=ws, fn=fn, inc=inc, sem=sem):
            for (s, v) in ws:
                e.wait_ge(s, v)
            inst = fn(e)
            if inc:
                inst.then_inc(sem, inc)

        self.items.append(item)
        return tok

    def dma(self, deps, fn, inc=16):
        ch = self.rr % len(self.dch)
        self.rr += 1
        entry = self.dch[ch]
        sem_c, prev = entry[0], entry[1]
        deps = list(deps)
        if prev:
            deps.append((sem_c, prev))
        ws = self._prep(deps)
        entry[1] = prev + inc
        tok = (sem_c, entry[1])

        def item(e, ws=ws, fn=fn, sem_c=sem_c, inc=inc):
            for (s, v) in ws:
                e.wait_ge(s, v)
            fn(e).then_inc(sem_c, inc)

        self.items.append(item)
        return tok

    def waits(self, deps):
        ws = self._prep(deps)
        if not ws:
            return

        def item(e, ws=ws):
            for (s, v) in ws:
                e.wait_ge(s, v)

        self.items.append(item)


class TV:

    def __init__(self, h, c0, ncols, rows):
        self.h = h
        self.c0 = c0
        self.ncols = ncols
        self.rows = rows

    def __getitem__(self, key):
        ps, cs = key
        if self.rows == 1:
            ps = slice(0, 1)
        start = cs.start if cs.start is not None else 0
        stop = cs.stop if cs.stop is not None else self.ncols
        return self.h[ps, self.c0 + start:self.c0 + stop]

    def rr(self, spec, **kw):
        return self.h[:, self.c0:self.c0 + self.ncols].rearrange(spec, **kw)


class Arena:
    def __init__(self, h32):
        self.h = {F32: h32, BF16: h32.bitcast(BF16), I32: h32.bitcast(I32)}
        self.mult = {F32: 1, BF16: 2, I32: 1}
        self.free = [[0, ARENA_W]]
        self.live = {}

    def alloc(self, name, elems, dt, rows=128):
        m = self.mult[dt]
        words = -(-elems // m)
        words = -(-words // 16) * 16
        for i, (off, n) in enumerate(self.free):
            if n >= words:
                if n == words:
                    self.free.pop(i)
                else:
                    self.free[i] = [off + words, n - words]
                self.live[name] = (off, words)
                return TV(self.h[dt], off * m, elems, rows)
        raise RuntimeError(f"arena OOM for {name}: need {words}w, free={self.free}")

    def view(self, name, elems, dt, rows=128, word_off=0):
        off, words = self.live[name]
        m = self.mult[dt]
        return TV(self.h[dt], (off + word_off) * m, elems, rows)

    def release(self, name):
        off, words = self.live.pop(name)
        self.free.append([off, words])
        self.free.sort()
        merged = []
        for seg in self.free:
            if merged and merged[-1][0] + merged[-1][1] == seg[0]:
                merged[-1][1] += seg[1]
            else:
                merged.append(list(seg))
        self.free = merged


def build_nc():
    nc = bass.Bass("TRN2", target_bir_lowering=False, debug=False)

    xT = nc.dram_tensor("xT", [D, T], F32, kind="ExternalInput").ap()
    wqT = nc.dram_tensor("wqT", [D, D], F32, kind="ExternalInput").ap()
    wvT = nc.dram_tensor("wvT", [D, D], F32, kind="ExternalInput").ap()
    wgT = nc.dram_tensor("wgT", [D, D], F32, kind="ExternalInput").ap()
    woT = nc.dram_tensor("woT", [D, D], F32, kind="ExternalInput").ap()
    wg_s = nc.dram_tensor("wg_s", [II, D], F32, kind="ExternalInput").ap()
    wu_s = nc.dram_tensor("wu_s", [II, D], F32, kind="ExternalInput").ap()
    wd_s = nc.dram_tensor("wd_s", [D, II], F32, kind="ExternalInput").ap()
    beta_pc = nc.dram_tensor("beta_pc", [128, DC], F32, kind="ExternalInput").ap()
    gam_pc = nc.dram_tensor("gam_pc", [128, DC], F32, kind="ExternalInput").ap()
    bln_pc = nc.dram_tensor("bln_pc", [128, DC], F32, kind="ExternalInput").ap()
    clip2 = nc.dram_tensor("clip2", [128, 2], F32, kind="ExternalInput").ap()

    outT = nc.dram_tensor("outT", [D, T], F32, kind="ExternalOutput").ap()

    h_dram = nc.dram_tensor("h_dram", [II, T], F32).ap()
    xn_dram = nc.dram_tensor("xn_dram", [D, T], F32).ap()
    rows_bounce = nc.dram_tensor("rows_bounce", [4, T], F32).ap()
    bx_dram = nc.dram_tensor("bx_dram", [128, 1], F32).ap()
    bh_dram = nc.dram_tensor("bh_dram", [128, 1], F32).ap()
    ccx_in = nc.dram_tensor("ccx_in", [1, 1], F32).ap()
    ccx_out = nc.dram_tensor("ccx_out", [1, 1], F32, addr_space="Shared").ap()
    cch_in = nc.dram_tensor("cch_in", [1, 1], F32).ap()
    cch_out = nc.dram_tensor("cch_out", [1, 1], F32, addr_space="Shared").ap()

    es = contextlib.ExitStack()
    with es:
        sems = {n: es.enter_context(nc.semaphore("sem_" + n))
                for n in ["pe", "act", "dve", "pool", "syn"]}
        dsem = {q: [es.enter_context(nc.semaphore(f"dma_{q}{i}")) for i in range(k)]
                for q, k in [("syn", 6), ("act", 4), ("pool", 3)]}
        pe = Eng("pe", "tensor", sems["pe"])
        act = Eng("act", "scalar", sems["act"], dsem["act"])
        dve = Eng("dve", "vector", sems["dve"])
        pool = Eng("pool", "gpsimd", sems["pool"], dsem["pool"])
        syn = Eng("syn", "sync", sems["syn"], dsem["syn"])
        engines = [pe, act, dve, pool, syn]

        def flush(tag):
            used = [e for e in engines if e.items]
            if not used:
                return
            with nc.Block() as blk:
                for e in used:
                    items = e.items
                    e.items = []
                    getattr(blk, e.blockname)(
                        lambda eng, items=items: [it(eng) for it in items])

        def softbar(tag=None):
            toks = [(e.sem, e.n) for e in engines]
            for e in engines:
                toks += [(c[0], c[1]) for c in e.dch if c[1]]
            for e in engines:
                if e is pe and tag not in PE_FULL_TAGS:
                    continue
                e.waits([t for t in toks if t[0] is not e.sem])

        def SB(name, shape, dt):
            return es.enter_context(nc.sbuf_tensor(name, shape, dt))

        ones_f32r = SB("ones_f32r", [1, 128], F32)
        ones_bf16c = SB("ones_bf16c", [128, 1], BF16)
        ones_bf16r = SB("ones_bf16r", [1, 128], BF16)
        zero_col = SB("zero_col", [128, 1], F32)
        eps_col = SB("eps_col", [128, 1], F32)
        clip_sb = SB("clip_sb", [128, 2], F32)
        sig_sb = SB("sig_sb", [128, 2], F32)
        betab = SB("betab", [128, DC], F32)
        lamcols = SB("lamcols", [128, DC], F32)
        gcols = SB("gcols", [128, DC], F32)
        bcols = SB("bcols", [128, DC], F32)
        scl = SB("scl", [128, 64], F32)
        g11 = SB("g11", [1, 4], F32)
        gm_row = SB("gm_row", [1, 128], F32)

        arena_h = es.enter_context(nc.sbuf_tensor("arena", [128, ARENA_W], F32))
        ar = Arena(arena_h)
        psumA = es.enter_context(nc.psum_tensor("psumA", [128, T], F32))
        psumB = es.enter_context(nc.psum_tensor("psumB", [128, T], F32))

        ones_big = ar.alloc("ones_big", T, F32)
        wstage = [ar.alloc(f"wstg{i}", D, F32) for i in range(2)]

        t_c2 = syn.dma([], lambda e: e.dma_start(out=clip_sb[:, :], in_=clip2))
        t_bp = syn.dma([], lambda e: e.dma_start(out=betab[:, :], in_=beta_pc))
        t_gp = syn.dma([], lambda e: e.dma_start(out=gcols[:, :], in_=gam_pc))
        t_bl = syn.dma([], lambda e: e.dma_start(out=bcols[:, :], in_=bln_pc))
        t_ones = dve.op([], lambda e: e.memset(ones_big[:, :], 1.0))
        t_o32 = dve.op([], lambda e: e.memset(ones_f32r[:, :], 1.0))
        t_obc = dve.op([], lambda e: e.memset(ones_bf16c[:, :], 1.0))
        t_obr = dve.op([], lambda e: e.memset(ones_bf16r[:, :], 1.0))
        t_zc = dve.op([], lambda e: e.memset(zero_col[:, :], 0.0))
        t_ec = dve.op([], lambda e: e.memset(eps_col[:, :], LN_EPS))
        t_sig = act.op([t_c2], lambda e: e.activation(sig_sb[:, :], clip_sb[:, :], AF.Sigmoid))
        t_lam = act.op([t_bp], lambda e: e.activation(lamcols[:, :], betab[:, :], AF.Sigmoid))

        xnb = [ar.alloc(f"xnb{d}", T, BF16) for d in range(DC)]
        wv_sb = [ar.alloc(f"wv{d}", D, BF16) for d in range(DC)]
        wq_sb = [ar.alloc(f"wq{d}", D, BF16) for d in range(DC)]

        t_wload = {}
        t_wstage_rd = {}
        wsi = 0

        def emit_wpair(d):
            nonlocal wsi
            for wname, src, dst, ceng in [("wv", wvT, wv_sb, act),
                                          ("wq", wqT, wq_sb, dve)]:
                b = wsi % 2
                tl = pool.dma([t_wstage_rd.get(wsi - 2)],
                              lambda e, s=src, d=d, b=b: e.dma_start(
                                  out=wstage[b][:, :], in_=s[d * 128:(d + 1) * 128, :]))
                if ceng is act:
                    tc = act.op([tl], lambda e, d=d, b=b, dst=dst: e.copy(
                        dst[d][:, :], wstage[b][:, :]))
                else:
                    tc = dve.op([tl], lambda e, d=d, b=b, dst=dst: e.tensor_copy(
                        dst[d][:, :], wstage[b][:, :]))
                t_wload[(wname, d)] = tc
                t_wstage_rd[wsi] = tc
                wsi += 1

        x_stage = [ar.alloc(f"xstg{i}", T, F32) for i in range(3)]
        M_b = ar.alloc("M_b", T, F32)
        R_b = ar.alloc("R_b", T, F32)
        scrX = [ar.alloc(f"scrX{i}", T, F32) for i in range(2)]
        xnX = [ar.alloc(f"xnX{i}", T, F32) for i in range(2)]
        xb2 = [ar.alloc(f"xb2_{i}", T, BF16) for i in range(2)]
        xq2 = [ar.alloc(f"xq2_{i}", T, BF16) for i in range(2)]
        rows = [ar.alloc(f"rowA{i}", T, F32, rows=1) for i in range(4)]
        rA, rB, rC, rD = rows

        t_x1 = {}
        t_mmA = {}
        if True:
            ps_sx = psumA[0:1, :]
            ps_sx2 = psumB[0:1, :]
            tok_stats = None
            for d in range(DC):
                b = d % 2
                b3 = d % 3
                t_x1[d] = syn.dma([t_mmA.get(d - 3)], lambda e, d=d, b3=b3: e.dma_start(
                    out=x_stage[b3][:, :], in_=xT[d * 128:(d + 1) * 128, :]))
                tcb = act.op([t_x1[d], t_mmA.get(d - 2)], lambda e, b=b, b3=b3: e.copy(
                    xb2[b][:, :], x_stage[b3][:, :]))
                tcs = dve.op([t_x1[d], t_mmA.get(d - 2)],
                             lambda e, b=b, b3=b3: e.tensor_mul(
                                 xq2[b][:, :], x_stage[b3][:, :], x_stage[b3][:, :]))
                emit_wpair(d)
                pe.waits([tcb, tcs, t_obc])
                for tb in range(NTB):
                    pe.op([], lambda e, b=b, tb=tb, d=d: e.matmul(
                        ps_sx[0:1, tb * TBW:(tb + 1) * TBW], ones_bf16c[:, :],
                        xb2[b][:, tb * TBW:(tb + 1) * TBW],
                        start=(d == 0), stop=(d == DC - 1)), inc=0)
                    last = (tb == NTB - 1)
                    tmm = pe.op([], lambda e, b=b, tb=tb, d=d: e.matmul(
                        ps_sx2[0:1, tb * TBW:(tb + 1) * TBW], ones_bf16c[:, :],
                        xq2[b][:, tb * TBW:(tb + 1) * TBW],
                        start=(d == 0), stop=(d == DC - 1)),
                        inc=1 if last else 0)
                    if last:
                        t_mmA[d] = tmm
                        tok_stats = tmm
            t_ev1 = act.op([tok_stats], lambda e: e.copy(rA[:, :], ps_sx[:, :]))
            t_ev2 = act.op([tok_stats], lambda e: e.copy(rB[:, :], ps_sx2[:, :]))
            t_mu = dve.op([t_ev1], lambda e: e.tensor_scalar(
                rC[:, :], rA[:, :], 1.0 / D, None, ALU.mult))
            t_e2 = dve.op([t_ev2, t_mu], lambda e: e.tensor_scalar(
                rA[:, :], rB[:, :], 1.0 / D, None, ALU.mult))
            t_m2 = act.op([t_mu, t_ev2, t_e2], lambda e: e.activation(
                rB[:, :], rC[:, :], AF.Square))
            t_var = dve.op([t_e2, t_m2], lambda e: e.tensor_sub(
                rD[:, :], rA[:, :], rB[:, :]))
            t_sd = act.op([t_var, t_ec, t_e2], lambda e: e.activation(
                rA[:, :], rD[:, :], AF.Sqrt, bias=eps_col[0:1, 0:1], scale=1.0))
            t_rstd = dve.op([t_sd, t_var], lambda e: e.reciprocal(rD[:, :], rA[:, :]))

        if True:
            def bc128(row_ap):
                return bass.AP(row_ap.tensor, row_ap.offset,
                               [[0, 128]] + [list(dd) for dd in row_ap.ap[1:]])

            t_sp1 = act.dma([t_mu], lambda e: e.dma_start(
                out=rows_bounce[0:1, :], in_=rC[:, :]))
            t_sp2 = syn.dma([t_rstd], lambda e: e.dma_start(
                out=rows_bounce[1:2, :], in_=rD[:, :]))
            t_Mb1 = act.dma([t_sp1], lambda e: e.dma_start(
                out=M_b[:, 0:1024], in_=bc128(rows_bounce[0:1, 0:1024])))
            t_Mb2 = syn.dma([t_sp1], lambda e: e.dma_start(
                out=M_b[:, 1024:2048], in_=bc128(rows_bounce[0:1, 1024:2048])))
            t_Rb1 = syn.dma([t_sp2], lambda e: e.dma_start(
                out=R_b[:, 0:1024], in_=bc128(rows_bounce[1:2, 0:1024])))
            t_Rb2 = act.dma([t_sp2], lambda e: e.dma_start(
                out=R_b[:, 1024:2048], in_=bc128(rows_bounce[1:2, 1024:2048])))
            t_Mb = t_Mb2
            t_Rb = t_Rb2

            gslot = 0
            t_gm = None
            prev = {}
            for d in range(DC):
                b = d % 2
                b3 = d % 3
                t_x2 = syn.dma([prev.get(("sub", d - 3)), tok_stats],
                               lambda e, d=d, b3=b3: e.dma_start(
                                   out=x_stage[b3][:, :],
                                   in_=xT[d * 128:(d + 1) * 128, :]))
                s2eng = pool if d in (2, 5) else dve
                t_sub = s2eng.op([t_x2, t_Mb, t_Mb1, prev.get(("mul", d - 2))],
                                 lambda e, b=b, b3=b3: e.tensor_sub(
                                     scrX[b][:, :], x_stage[b3][:, :], M_b[:, :]))
                prev[("sub", d)] = t_sub
                t_xn = s2eng.op([t_sub, t_Rb, t_Rb1, prev.get(("cast", d - 2)),
                                 prev.get(("rm", d - 2)), prev.get(("spill", d - 2))],
                                lambda e, b=b: e.tensor_mul(
                                    xnX[b][:, :], scrX[b][:, :], R_b[:, :]))
                prev[("mul", d)] = t_xn
                prev[("cast", d)] = act.op([t_xn], lambda e, d=d, b=b: e.copy(
                    xnb[d][:, :], xnX[b][:, :]))
                rm = dve.op([t_xn], lambda e, d=d, b=b: e.tensor_reduce(
                    scl[:, 2 + b:3 + b], xnX[b][:, :], AXX, ALU.max,
                    apply_absolute_value=True))
                prev[("rm", d)] = rm
                prev[("spill", d)] = act.dma([t_xn], lambda e, d=d, b=b: e.dma_start(
                    out=xn_dram[d * 128:(d + 1) * 128, :], in_=xnX[b][:, :]))
                if d == 0:
                    t_gm = dve.op([rm], lambda e: e.tensor_copy(scl[:, 0:1], scl[:, 2:3]))
                else:
                    ns = 1 - gslot
                    t_gm = dve.op([rm, t_gm],
                                  lambda e, b=b, g=gslot, ns=ns: e.tensor_tensor(
                                      scl[:, ns:ns + 1], scl[:, g:g + 1],
                                      scl[:, 2 + b:3 + b], ALU.max))
                    gslot = ns
            t_xnb_last = prev[("cast", DC - 1)]
        for nm in (["xstg0", "xstg1", "xstg2", "M_b", "R_b", "scrX0", "scrX1", "xnX0",
                    "xnX1", "xb2_0", "xb2_1", "xq2_0", "xq2_1",
                    "rowA0", "rowA1", "rowA2", "rowA3"]):
            ar.release(nm)

        softbar("C1")
        i32scr = ar.alloc("i32scr", T, I32)
        kx = [ar.alloc(f"kx{d}", T, BF16) for d in range(DC)]
        qscr = ar.alloc("qscr", T, F32)
        xn_stage = [ar.alloc(f"xnstg{i}", T, F32) for i in range(2)]
        lamb = ar.alloc("lamb", T, F32)
        v_sb = ar.alloc("v_sb", T, F32)
        states = ar.alloc("states", T, F32)
        yb = [ar.alloc(f"yb{d}", T, BF16) for d in range(DC)]

        t_b1 = pool.dma([t_gm], lambda e, g=gslot: e.dma_start(
            out=bx_dram, in_=scl[:, g:g + 1]))
        t_b2 = pool.dma([t_b1], lambda e: e.dma_start(
            out=gm_row[:, :], in_=bx_dram.rearrange("a b -> b a")))
        t_gx = dve.op([t_b2], lambda e: e.tensor_reduce(
            g11[:, 0:1], gm_row[:, :], AXX, ALU.max))
        t_cx1 = pool.dma([t_gx], lambda e: e.dma_start(out=ccx_in, in_=g11[:, 0:1]))
        t_cc1 = pool.dma([t_cx1], lambda e: e.collective_compute(
            "AllReduce", ALU.max, replica_groups=[list(range(NCORES))],
            ins=[ccx_in.opt()], outs=[ccx_out.opt()]), inc=1)
        t_cx2 = pool.dma([t_cc1], lambda e: e.dma_start(
            out=g11[:, 0:1], in_=ccx_out))

        t_grow = pool.op([t_cx2, t_o32], lambda e: e.tensor_single_scalar(
            gm_row[:, :], ones_f32r[:, :], g11[0:1, 0:1], ALU.mult))
        t_b3 = pool.dma([t_grow], lambda e: e.dma_start(
            out=bx_dram.rearrange("a b -> b a"), in_=gm_row[:, :]))
        t_b4 = pool.dma([t_b3], lambda e: e.dma_start(
            out=scl[:, 6:7], in_=bx_dram))
        t_t1c = pool.op([t_b4, t_sig], lambda e: e.tensor_mul(
            scl[:, 7:8], scl[:, 6:7], sig_sb[:, 0:1]))
        t_sxc = pool.op([t_t1c], lambda e: e.tensor_scalar(
            scl[:, 5:6], scl[:, 7:8], R127, None, ALU.mult))

        if True:
            v_ps = psumA[:, :]
            q_ps = psumB[:, :]
            t_scan = None
            t_vev = None
            t_yTT = None
            t_yTTs = {}
            t_rxc = None
            prevq = {}
            for c in range(DC):
                t_lamb = act.op([t_lam, t_ones, t_zc, t_scan],
                                lambda e, c=c: e.activation(
                                    lamb[:, :], ones_big[:, :], AF.Copy,
                                    bias=0.0, scale=lamcols[:, c:c + 1]))
                pe.waits([t_vev, t_Rb, t_xnb_last])
                tok_v = None
                for d in range(DC):
                    for tb in range(NTB):
                        lastv = (d == DC - 1 and tb == NTB - 1)
                        tok_v = pe.op([], lambda e, c=c, d=d, tb=tb: e.matmul(
                            v_ps[:, tb * TBW:(tb + 1) * TBW],
                            wv_sb[d][:, c * 128:(c + 1) * 128],
                            xnb[d][:, tb * TBW:(tb + 1) * TBW],
                            start=(d == 0), stop=(d == DC - 1)),
                            inc=1 if lastv else 0)
                t_vev = act.op([tok_v, t_scan], lambda e: e.copy(v_sb[:, :], v_ps[:, :]))
                t_scan = dve.op([t_vev, t_lamb, t_yTT],
                                lambda e: e.tensor_tensor_scan(
                                    states[:, :], lamb[:, :], v_sb[:, :], 0.0,
                                    ALU.mult, ALU.add))
                pe.waits([t_yTT])
                tok_q = None
                for d in range(DC):
                    for tb in range(NTB):
                        lastq = (d == DC - 1 and tb == NTB - 1)
                        tok_q = pe.op([], lambda e, c=c, d=d, tb=tb: e.matmul(
                            q_ps[:, tb * TBW:(tb + 1) * TBW],
                            wq_sb[d][:, c * 128:(c + 1) * 128],
                            xnb[d][:, tb * TBW:(tb + 1) * TBW],
                            start=(d == 0), stop=(d == DC - 1)),
                            inc=1 if lastq else 0)
                t_yTT = dve.op([tok_q, t_scan], lambda e, c=c: e.tensor_mul(
                    yb[c][:, :], q_ps[:, :], states[:, :]))
                t_yTTs[c] = t_yTT

                if c >= 4:
                    if c == 4:
                        t_rxc = dve.op([t_sxc], lambda e: e.reciprocal(
                            scl[:, 4:5], scl[:, 5:6]))
                    for d in (2 * (c - 4), 2 * (c - 4) + 1):
                        bq = d % 2
                        t_xl = syn.dma([prevq.get(("ts1", d - 2))],
                                       lambda e, d=d, bq=bq: e.dma_start(
                                           out=xn_stage[bq][:, :],
                                           in_=xn_dram[d * 128:(d + 1) * 128, :]))
                        t_q1 = dve.op([t_xl, t_rxc, prevq.get(("ts2", d - 1))],
                                      lambda e, bq=bq: e.tensor_scalar(
                                          qscr[:, :], xn_stage[bq][:, :],
                                          scl[:, 4:5], CLIP_HI, ALU.mult, ALU.min))
                        prevq[("ts1", d)] = t_q1
                        t_q2 = dve.op([t_q1, prevq.get(("cp", d - 1))],
                                      lambda e: e.tensor_scalar(
                                          i32scr[:, :], qscr[:, :], CLIP_LO,
                                          None, ALU.max))
                        prevq[("ts2", d)] = t_q2
                        prevq[("cp", d)] = dve.op([t_q2], lambda e, d=d: e.tensor_copy(
                            kx[d][:, :], i32scr[:, :]))
            wg_sb = [ar.view(f"wv{d}", D, BF16) for d in range(DC)]
            for d in range(DC):
                b = wsi % 2
                tl = syn.dma([t_wstage_rd.get(wsi - 2), tok_v],
                             lambda e, d=d, b=b: e.dma_start(
                                 out=wstage[b][:, :],
                                 in_=wgT[d * 128:(d + 1) * 128, :]))
                tc = act.op([tl], lambda e, d=d, b=b: e.copy(
                    wg_sb[d][:, :], wstage[b][:, :]))
                t_wload[("wg", d)] = tc
                t_wstage_rd[wsi] = tc
                wsi += 1
        for nm in (["lamb", "v_sb", "states", "qscr", "xnstg0", "xnstg1"] +
                   [f"wq{d}" for d in range(DC)]):
            ar.release(nm)

        softbar("C2")
        NM2 = ar.alloc("NM2", T, F32)
        R2 = ar.alloc("R2", T, F32)
        ysq = [ar.alloc(f"ysq{i}", T, BF16) for i in range(2)]
        rows = [ar.alloc(f"rw{i}", T, F32, rows=1) for i in range(4)]
        rA, rB, rC, rD = rows

        if True:
            ps_sy = psumA[0:1, :]
            ps_sy2 = psumA[32:33, :]
            t_ysq_war = {}
            tok_sy = None
            for d in range(DC):
                yq = d % 2
                t_ysq = act.op([t_yTTs[d], t_ysq_war.get(d - 2)],
                               lambda e, d=d, yq=yq: e.activation(
                                   ysq[yq][:, :], yb[d][:, :], AF.Square))
                pe.waits([t_ysq, t_obc, t_vev])
                for tb in range(NTB):
                    pe.op([], lambda e, d=d, tb=tb: e.matmul(
                        ps_sy[0:1, tb * TBW:(tb + 1) * TBW], ones_bf16c[:, :],
                        yb[d][:, tb * TBW:(tb + 1) * TBW],
                        start=(d == 0), stop=(d == DC - 1)), inc=0)
                    last = (tb == NTB - 1)
                    tok_sy = pe.op([], lambda e, d=d, tb=tb, yq=yq: e.matmul(
                        ps_sy2[0:1, tb * TBW:(tb + 1) * TBW], ones_bf16c[:, :],
                        ysq[yq][:, tb * TBW:(tb + 1) * TBW],
                        start=(d == 0), stop=(d == DC - 1)),
                        inc=1 if last else 0)
                t_ysq_war[d] = tok_sy

            t_ev1 = act.op([tok_sy], lambda e: e.copy(rA[:, :], ps_sy[:, :]))
            t_ev2 = act.op([tok_sy], lambda e: e.copy(rB[:, :], ps_sy2[:, :]))
            t_mu = dve.op([t_ev1], lambda e: e.tensor_scalar(
                rC[:, :], rA[:, :], 1.0 / D, None, ALU.mult))
            t_e2 = dve.op([t_ev2, t_mu], lambda e: e.tensor_scalar(
                rA[:, :], rB[:, :], 1.0 / D, None, ALU.mult))
            t_m2 = act.op([t_mu, t_ev2, t_e2], lambda e: e.activation(
                rB[:, :], rC[:, :], AF.Square))
            t_var = dve.op([t_e2, t_m2], lambda e: e.tensor_sub(
                rD[:, :], rA[:, :], rB[:, :]))
            t_sd = act.op([t_var, t_e2], lambda e: e.activation(
                rA[:, :], rD[:, :], AF.Sqrt, bias=eps_col[0:1, 0:1], scale=1.0))
            t_rstd = dve.op([t_sd, t_var], lambda e: e.reciprocal(rD[:, :], rA[:, :]))
            t_n1 = dve.op([t_rstd, t_sd], lambda e: e.tensor_mul(
                rA[:, :], rC[:, :], rD[:, :]))
            t_n2 = dve.op([t_n1, t_m2], lambda e: e.tensor_scalar(
                rB[:, :], rA[:, :], -1.0, None, ALU.mult))

        if True:
            def bc128(row_ap):
                return bass.AP(row_ap.tensor, row_ap.offset,
                               [[0, 128]] + [list(dd) for dd in row_ap.ap[1:]])

            t_sp3 = act.dma([t_rstd], lambda e: e.dma_start(
                out=rows_bounce[2:3, :], in_=rD[:, :]))
            t_sp4 = syn.dma([t_n2], lambda e: e.dma_start(
                out=rows_bounce[3:4, :], in_=rB[:, :]))
            t_R2 = act.dma([t_sp3], lambda e: e.dma_start(
                out=R2[:, :], in_=bc128(rows_bounce[2:3, :])))
            t_NM2 = syn.dma([t_sp4], lambda e: e.dma_start(
                out=NM2[:, :], in_=bc128(rows_bounce[3:4, :])))
        for nm in ["ysq0", "ysq1", "rw0", "rw1", "rw2", "rw3"]:
            ar.release(nm)

        softbar("C3")
        ga_sb = [ar.alloc(f"ga{i}", T, BF16) for i in range(2)]
        tscr1p = [ar.alloc(f"tscr1{i}", T, BF16) for i in range(2)]
        tscr2p = [ar.alloc(f"tscr2{i}", T, BF16) for i in range(2)]
        if True:
            t_sil = {}
            t_yg = {}
            tt3_war = {}
            for c in range(DC):
                b = c % 2
                ga_ps = psumA[:, :] if b == 0 else psumB[:, :]
                pe.waits([t_sil.get(c - 2), t_ev1, t_ev2, t_yTT])
                tok_g = None
                for d in range(DC):
                    for tb in range(NTB):
                        lastg = (d == DC - 1 and tb == NTB - 1)
                        tok_g = pe.op([t_wload[("wg", d)]],
                                      lambda e, c=c, d=d, tb=tb, ga_ps=ga_ps: e.matmul(
                                          ga_ps[:, tb * TBW:(tb + 1) * TBW],
                                          wg_sb[d][:, c * 128:(c + 1) * 128],
                                          xnb[d][:, tb * TBW:(tb + 1) * TBW],
                                          start=(d == 0), stop=(d == DC - 1)),
                                      inc=1 if lastg else 0)
                t_sil[c] = act.op([tok_g, t_yg.get(c - 2)],
                                  lambda e, b=b, ga_ps=ga_ps: e.activation(
                                      ga_sb[b][:, :], ga_ps[:, :], AF.Silu))
                tt1 = dve.op([t_R2, t_yg.get(c - 2)], lambda e, c=c, b=b: e.tensor_mul(
                    tscr1p[b][:, :], yb[c][:, :], R2[:, :]))
                tt2 = dve.op([tt1, t_NM2, tt3_war.get(c - 2)],
                             lambda e, b=b: e.tensor_add(
                                 tscr2p[b][:, :], tscr1p[b][:, :], NM2[:, :]))
                tt3 = dve.op([tt2, t_gp, t_bl, t_yg.get(c - 2)],
                             lambda e, c=c, b=b: e.tensor_scalar(
                                 tscr1p[b][:, :], tscr2p[b][:, :], gcols[:, c:c + 1],
                                 bcols[:, c:c + 1], ALU.mult, ALU.add))
                tt3_war[c] = tt3
                t_yg[c] = dve.op([tt3, t_sil[c]], lambda e, c=c, b=b: e.tensor_mul(
                    yb[c][:, :], tscr1p[b][:, :], ga_sb[b][:, :]))
        for nm in (["ga0", "ga1", "tscr10", "tscr11", "tscr20", "tscr21", "NM2", "R2"] +
                   [f"wv{d}" for d in range(DC)] + [f"xnb{d}" for d in range(DC)]):
            ar.release(nm)

        softbar("B")
        wo_sb = [ar.alloc(f"wo{d}", D, BF16) for d in range(DC)]
        NPB = 3
        ws_g = [ar.alloc(f"wsg{i}", D, F32) for i in range(NPB)]
        ws_u = [ar.alloc(f"wsu{i}", D, F32) for i in range(NPB)]
        ktm_g = [ar.alloc(f"ktmg{i}", D, BF16) for i in range(NPB)]
        ktm_u = [ar.alloc(f"ktmu{i}", D, BF16) for i in range(NPB)]
        kwT_g = [ar.alloc(f"kwTg{i}", DC * 128, BF16) for i in range(NPB)]
        kwT_u = [ar.alloc(f"kwTu{i}", DC * 128, BF16) for i in range(NPB)]
        h_tile = [ar.alloc(f"htile{i}", T, F32) for i in range(2)]
        sg_scr = [ar.alloc(f"sgscr{i}", TBW, F32) for i in range(2)]
        up_scr = [ar.alloc(f"upscr{i}", TBW, F32) for i in range(2)]

        if True:
            ps_g = psumA[:, :]
            ps_u = psumB[:, :]
            tok_c3_sil = (act.sem, act.n)
            for d in range(DC):
                b = wsi % 2
                tl = syn.dma([t_wstage_rd.get(wsi - 2)],
                             lambda e, d=d, b=b: e.dma_start(
                                 out=wstage[b][:, :], in_=woT[d * 128:(d + 1) * 128, :]))
                tc = act.op([tl], lambda e, d=d, b=b: e.copy(
                    wo_sb[d][:, :], wstage[b][:, :]))
                t_wload[("wo", d)] = tc
                t_wstage_rd[wsi] = tc
                wsi += 1

            tok_wread = {}
            tok_k = {}
            tok_scg = {}
            tok_tp = {}
            tok_epi = {}
            tok_hdma = [None, None]
            tok_hread = [None, None]
            state = {"i32rd": None}
            t_hm = None
            hslot = 0

            def prep_b(ci):
                b = ci % NPB
                tlg = syn.dma([tok_wread.get(("g", ci - NPB))],
                              lambda e, ci=ci, b=b: e.dma_start(
                                  out=ws_g[b][:, :], in_=wg_s[ci * 128:(ci + 1) * 128, :]))
                tlu = syn.dma([tok_wread.get(("u", ci - NPB))],
                              lambda e, ci=ci, b=b: e.dma_start(
                                  out=ws_u[b][:, :], in_=wu_s[ci * 128:(ci + 1) * 128, :]))
                for w, tl, ws_, ktm_, kwT_, sm0, cg0, rw0 in [
                        ("g", tlg, ws_g, ktm_g, kwT_g, 32, 38, 44),
                        ("u", tlu, ws_u, ktm_u, kwT_u, 35, 41, 47)]:
                    sm = sm0 + b
                    cg = cg0 + b
                    rw = rw0 + b
                    t_red = dve.op([tl], lambda e, b=b, ws_=ws_, sm=sm: e.tensor_reduce(
                        scl[:, sm:sm + 1], ws_[b][:, :], AXX, ALU.max,
                        apply_absolute_value=True))
                    t_sw = dve.op([t_red], lambda e, sm=sm: e.tensor_scalar(
                        scl[:, sm:sm + 1], scl[:, sm:sm + 1], R127, None, ALU.mult))
                    t_rw = dve.op([t_sw], lambda e, sm=sm, rw=rw: e.reciprocal(
                        scl[:, rw:rw + 1], scl[:, sm:sm + 1]))
                    tok_scg[(w, ci)] = dve.op(
                        [t_sw, t_sxc, tok_epi.get((w, NTB - 1, ci - NPB))],
                        lambda e, sm=sm, cg=cg: e.tensor_mul(
                            scl[:, cg:cg + 1], scl[:, sm:sm + 1], scl[:, 5:6]))
                    t_i = dve.op([t_rw, state["i32rd"]],
                                 lambda e, b=b, ws_=ws_, rw=rw: e.tensor_scalar(
                                     i32scr[:, 0:D], ws_[b][:, :],
                                     scl[:, rw:rw + 1], None, ALU.mult))
                    tok_wread[(w, ci)] = t_i
                    t_k = dve.op([t_i, tok_tp.get((w, ci - NPB))],
                                 lambda e, b=b, ktm_=ktm_: e.tensor_copy(
                                     ktm_[b][:, :], i32scr[:, 0:D]))
                    state["i32rd"] = t_k
                    tok_k[(w, ci)] = t_k
                    tok_tp[(w, ci)] = act.dma(
                        [t_k] + [tok_epi.get((w, tb, ci - NPB)) for tb in range(NTB)],
                        lambda e, b=b, ktm_=ktm_, kwT_=kwT_: e.dma_start_transpose(
                            kwT_[b].rr("p (a b) -> p a b", a=DC), ktm_[b][:, :]))

            for pp in range(NPB):
                prep_b(pp)
            for ci in range(CI):
                b = ci % NPB
                b2 = ci % 2
                for half in range(2):
                    tbs = (2 * half, 2 * half + 1)
                    pe.waits([tok_tp[("g", ci)], tok_tp[("u", ci)], tok_c3_sil])
                    pe.waits([tok_epi.get(("g", tb, ci - 1)) for tb in tbs] +
                             [tok_epi.get(("u", tb, ci - 1)) for tb in tbs] +
                             [tok_epi.get(("p", tb, ci - 1)) for tb in tbs])
                    tok_mm = None
                    for hc in range(DC):
                        for w, kwT_, ps in [("g", kwT_g, ps_g), ("u", kwT_u, ps_u)]:
                            for tb in tbs:
                                lastm = (hc == DC - 1 and w == "u" and tb == tbs[-1])
                                tok_mm = pe.op([], lambda e, b=b, hc=hc, kwT_=kwT_, ps=ps, tb=tb: e.matmul(
                                    ps[:, tb * TBW:(tb + 1) * TBW],
                                    kwT_[b][:, hc * 128:(hc + 1) * 128],
                                    kx[hc][:, tb * TBW:(tb + 1) * TBW],
                                    start=(hc == 0), stop=(hc == DC - 1)),
                                    inc=1 if lastm else 0)
                    for tb in tbs:
                        eb = tb % 2
                        war_sg = (tok_epi.get(("p", tb - 2, ci)) or
                                  tok_epi.get(("p", tb + 2, ci - 1)))
                        t_sg = act.op([tok_mm, tok_scg[("g", ci)], t_zc, war_sg],
                                      lambda e, b=b, tb=tb, eb=eb: e.activation(
                                          sg_scr[eb][:, :],
                                          ps_g[:, tb * TBW:(tb + 1) * TBW],
                                          AF.Silu, bias=zero_col[:, 0:1],
                                          scale=scl[:, 38 + b:39 + b]))
                        t_up = act.op([tok_mm, tok_scg[("u", ci)], war_sg],
                                      lambda e, b=b, tb=tb, eb=eb: e.activation(
                                          up_scr[eb][:, :],
                                          ps_u[:, tb * TBW:(tb + 1) * TBW],
                                          AF.Copy, bias=0.0,
                                          scale=scl[:, 41 + b:42 + b]))
                        tok_epi[("g", tb, ci)] = t_sg
                        tok_epi[("u", tb, ci)] = t_up
                        tok_epi[("p", tb, ci)] = dve.op(
                            [t_sg, t_up, tok_hdma[b2], tok_hread[b2]],
                            lambda e, b2=b2, tb=tb, eb=eb: e.tensor_mul(
                                h_tile[b2][:, tb * TBW:(tb + 1) * TBW],
                                sg_scr[eb][:, :], up_scr[eb][:, :]))
                t_hfull = [tok_epi[("p", tb, ci)] for tb in range(NTB)]
                rm = dve.op(t_hfull, lambda e, b2=b2: e.tensor_reduce(
                    scl[:, 2 + b2:3 + b2], h_tile[b2][:, :], AXX, ALU.max,
                    apply_absolute_value=True))
                if ci == 0:
                    t_hm = dve.op([rm], lambda e: e.tensor_copy(scl[:, 0:1], scl[:, 2:3]))
                    hslot = 0
                else:
                    ns = 1 - hslot
                    t_hm = dve.op([rm, t_hm],
                                  lambda e, b2=b2, g=hslot, ns=ns: e.tensor_tensor(
                                      scl[:, ns:ns + 1], scl[:, g:g + 1],
                                      scl[:, 2 + b2:3 + b2], ALU.max))
                    hslot = ns
                tok_hread[b2] = rm
                if ci < CI - 2:
                    tok_hdma[b2] = pool.dma(t_hfull, lambda e, ci=ci, b2=b2: e.dma_start(
                        out=h_dram[ci * 128:(ci + 1) * 128, :], in_=h_tile[b2][:, :]))
                if ci + NPB < CI:
                    prep_b(ci + NPB)
        for nm in ([f"wsg{i}" for i in range(3)] + [f"wsu{i}" for i in range(3)] +
                   [f"ktmg{i}" for i in range(3)] + [f"ktmu{i}" for i in range(3)] +
                   [f"kwTg{i}" for i in range(3)] + [f"kwTu{i}" for i in range(3)] +
                   ["sgscr0", "sgscr1", "upscr0", "upscr1",
                    "i32scr"] + [f"kx{d}" for d in range(DC)]):
            ar.release(nm)

        softbar("B2")
        s_sb = [ar.alloc(f"ssb{d}", T, BF16) for d in range(DC)]
        kwdT = [ar.alloc(f"kwdT{j}", CI * 128, BF16) for j in range(DC)]
        wd_stage = [ar.alloc(f"wds{i}", II, F32) for i in range(2)]
        ktm_wd = [ar.alloc(f"ktmwd{i}", II, BF16) for i in range(2)]
        i32b2 = ar.alloc("i32b2", 1408, I32)
        wd32m_a = ar.alloc("wd32m_a", 1152, F32)
        wd32m_b = ar.alloc("wd32m_b", 256, F32)

        t_b1 = pool.dma([t_hm], lambda e, g=hslot: e.dma_start(
            out=bh_dram, in_=scl[:, g:g + 1]))
        t_b2 = pool.dma([t_b1], lambda e: e.dma_start(
            out=gm_row[:, :], in_=bh_dram.rearrange("a b -> b a")))
        t_gh = dve.op([t_b2], lambda e: e.tensor_reduce(
            g11[:, 0:1], gm_row[:, :], AXX, ALU.max))
        t_ch1 = pool.dma([t_gh], lambda e: e.dma_start(out=cch_in, in_=g11[:, 0:1]))
        t_cc2 = pool.dma([t_ch1], lambda e: e.collective_compute(
            "AllReduce", ALU.max, replica_groups=[list(range(NCORES))],
            ins=[cch_in.opt()], outs=[cch_out.opt()]), inc=1)
        t_ch2 = pool.dma([t_cc2], lambda e: e.dma_start(
            out=g11[:, 0:1], in_=cch_out))

        if True:
            s_ps = psumA[:, 0:2 * TBW]
            tok_b_act = (act.sem, act.n)
            t_sev = {}
            seq = 0
            for ds in range(DC):
                for tb in range(NTB):
                    half = seq % 2
                    pe.waits([t_sev.get(seq - 2), tok_b_act])
                    tok_s = None
                    for d in range(DC):
                        tok_s = pe.op([t_wload[("wo", d)], t_yg[d]],
                                      lambda e, ds=ds, d=d, tb=tb, half=half: e.matmul(
                                          s_ps[:, half * TBW:(half + 1) * TBW],
                                          wo_sb[d][:, ds * 128:(ds + 1) * 128],
                                          yb[d][:, tb * TBW:(tb + 1) * TBW],
                                          start=(d == 0), stop=(d == DC - 1)),
                                      inc=1 if d == DC - 1 else 0)
                    t_sev[seq] = act.op([tok_s], lambda e, ds=ds, tb=tb, half=half: e.copy(
                        s_sb[ds][:, tb * TBW:(tb + 1) * TBW],
                        s_ps[:, half * TBW:(half + 1) * TBW]))
                    seq += 1

            t_wdq = {}
            t_wdi = {}
            t_wdtp = {}
            t_swd = {}
            prev_i = None
            prev_a1 = None
            t_a1_war = {}
            for jc in range(DC):
                b = jc % 2
                tl = syn.dma([t_wdq.get(jc - 2), t_wdi.get(jc - 2),
                              t_a1_war.get(jc - 2)],
                             lambda e, jc=jc, b=b: e.dma_start(
                    out=wd_stage[b][:, :], in_=wd_s[jc * 128:(jc + 1) * 128, :]))
                t_red = dve.op([tl], lambda e, b=b: e.tensor_reduce(
                    scl[:, 2:3], wd_stage[b][:, :], AXX, ALU.max,
                    apply_absolute_value=True))
                t_swd[jc] = dve.op([t_red], lambda e, jc=jc: e.tensor_scalar(
                    scl[:, 14 + jc:15 + jc], scl[:, 2:3], R127, None, ALU.mult))
                rws = 8
                t_rw = dve.op([t_swd[jc]],
                              lambda e, jc=jc, rws=rws: e.reciprocal(
                                  scl[:, rws:rws + 1], scl[:, 14 + jc:15 + jc]))
                kb2 = jc % 2
                t_i = dve.op([t_rw, prev_i],
                             lambda e, b=b: e.tensor_scalar(
                                 i32b2[:, :],
                                 wd_stage[b][:, 0:1408],
                                 scl[:, rws:rws + 1], None, ALU.mult))
                t_k0 = dve.op([t_i, t_wdtp.get(jc - 2)],
                              lambda e, kb2=kb2: e.tensor_copy(
                                  ktm_wd[kb2][:, 0:1408], i32b2[:, :]))
                prev_i = t_k0
                t_wdi[jc] = t_i
                t_i1 = dve.op([t_k0],
                              lambda e, b=b, rws=rws: e.tensor_scalar(
                                  i32b2[:, :],
                                  wd_stage[b][:, 1408:2816],
                                  scl[:, rws:rws + 1], None, ALU.mult))
                t_k1 = dve.op([t_i1, t_wdtp.get(jc - 2)],
                              lambda e, kb2=kb2: e.tensor_copy(
                                  ktm_wd[kb2][:, 1408:2816], i32b2[:, :]))
                prev_i = t_k1
                t_a1_war[jc] = t_k1
                tk_half = [t_k0, t_k1]
                t_wdq[jc] = tk_half[-1]
                tpq = act if jc % 2 == 0 else syn
                t_wdtp[jc] = tpq.dma(tk_half, lambda e, jc=jc, kb2=kb2: e.dma_start_transpose(
                    kwdT[jc].rr("p (a b) -> p a b", a=CI), ktm_wd[kb2][:, :]))

            t_grow2 = dve.op([t_ch2], lambda e: e.tensor_single_scalar(
                gm_row[:, :], ones_f32r[:, :], g11[0:1, 0:1], ALU.mult))
            t_b3h = pool.dma([t_grow2], lambda e: e.dma_start(
                out=bh_dram.rearrange("a b -> b a"), in_=gm_row[:, :]))
            t_b4h = pool.dma([t_b3h], lambda e: e.dma_start(
                out=scl[:, 6:7], in_=bh_dram))
            t_t1h = dve.op([t_b4h, t_sig], lambda e: e.tensor_mul(
                scl[:, 7:8], scl[:, 6:7], sig_sb[:, 1:2]))
            t_shc = dve.op([t_t1h], lambda e: e.tensor_scalar(
                scl[:, 31:32], scl[:, 7:8], R127, None, ALU.mult))
            t_rhc = dve.op([t_shc], lambda e: e.reciprocal(scl[:, 30:31], scl[:, 31:32]))
            t_s2 = {}
            for jc in range(DC):
                t_s2[jc] = dve.op([t_swd[jc], t_shc], lambda e, jc=jc: e.tensor_mul(
                    scl[:, 22 + jc:23 + jc], scl[:, 14 + jc:15 + jc],
                    scl[:, 31:32]))
        for nm in (["wds0", "wds1", "ktmwd0", "ktmwd1", "i32b2", "wd32m_a", "wd32m_b"] +
                   [f"wo{d}" for d in range(DC)] + [f"yb{d}" for d in range(DC)]):
            ar.release(nm)

        softbar("D")
        khtb = [ar.alloc(f"khtb{i}", CI * TBW, BF16) for i in range(2)]
        h_stage = [ar.alloc(f"hstg{i}", TBW, F32) for i in range(6)]
        ud_scr = [ar.alloc(f"udscr{i}", TBW, F32) for i in range(3)]
        i32d = [ar.alloc(f"i32d{i}", TBW, I32) for i in range(3)]
        g_scr = [ar.alloc(f"gscr{i}", TBW, BF16) for i in range(4)]
        sg_d = [ar.alloc(f"sgd{i}", TBW, F32) for i in range(4)]
        o_scr = [ar.alloc(f"oscr{i}", TBW, F32) for i in range(4)]
        xt_d = [ar.alloc(f"xtd{i}", TBW, F32) for i in range(4)]

        if True:
            dn_ps = [psumA[:, :], psumB[:, :]]
            tok_b2_act = (act.sem, act.n)
            tok_kh = {}
            tok_mmD = {}
            tok_oD = {}
            tok_gD = {}
            tok_sg_prev = {}
            tok_o_prev = {}
            tok_od_prev = {}

            ci_list = [CI - 2, CI - 1] + list(range(CI - 2))

            def quant_d(tb):
                kb = tb % 2
                for idx, ci in enumerate(ci_list):
                    s3 = tb * CI + idx
                    hb = s3 % 6
                    ub = s3 % 3
                    if ci >= CI - 2:
                        t_u = act.op([t_rhc, t_zc, tok_kh.get(s3 - 3)],
                                     lambda e, ci=ci, tb=tb, ub=ub: e.activation(
                                         ud_scr[ub][:, :],
                                         h_tile[ci % 2][:, tb * TBW:(tb + 1) * TBW],
                                         AF.Copy, bias=0.0, scale=scl[:, 30:31]))
                    else:
                        t_hl = syn.dma([tok_kh.get(s3 - 6)],
                                       lambda e, tb=tb, ci=ci, hb=hb: e.dma_start(
                                           out=h_stage[hb][:, :],
                                           in_=h_dram[ci * 128:(ci + 1) * 128,
                                                      tb * TBW:(tb + 1) * TBW]))
                        t_u = act.op([t_hl, t_rhc, t_zc, tok_kh.get(s3 - 3)],
                                     lambda e, hb=hb, ub=ub: e.activation(
                                         ud_scr[ub][:, :], h_stage[hb][:, :], AF.Copy,
                                         bias=0.0, scale=scl[:, 30:31]))
                    qeng = dve
                    t_i = qeng.op([t_u, tok_kh.get(s3 - 3)],
                                  lambda e, ub=ub: e.tensor_scalar(
                                      i32d[ub][:, :], ud_scr[ub][:, :],
                                      CLIP_LO, CLIP_HI, ALU.max, ALU.min))
                    tok_kh[s3] = qeng.op([t_i, tok_mmD.get((tb - 2, 3))],
                                         lambda e, kb=kb, ci=ci, ub=ub: e.tensor_copy(
                                             khtb[kb][:, ci * TBW:(ci + 1) * TBW],
                                             i32d[ub][:, :]))

            quant_d(0)
            for tb in range(NTB):
                kb = tb % 2
                if tb + 1 < NTB:
                    quant_d(tb + 1)

                for jq in range(4):
                    jcs = [2 * jq, 2 * jq + 1]
                    psrc = dn_ps[jq // 2]
                    pcol = (jq % 2) * 2 * TBW
                    pe.waits([tok_gD.get((tb - 1, jc)) for jc in jcs])
                    pe.waits([t_wdtp[jc] for jc in jcs] + [tok_b2_act])
                    tok_mm = None
                    for idx, ci in enumerate(ci_list):
                        pe.waits([tok_kh[tb * CI + idx]])
                        for jc in jcs:
                            lastm = (idx == CI - 1 and jc == jcs[-1])
                            tok_mm = pe.op([], lambda e, psrc=psrc, pcol=pcol, jc=jc,
                                           ci=ci, kb=kb, jq=jq, idx=idx: e.matmul(
                                psrc[:, pcol + (jc - 2 * jq) * TBW:
                                     pcol + (jc - 2 * jq + 1) * TBW],
                                kwdT[jc][:, ci * 128:(ci + 1) * 128],
                                khtb[kb][:, ci * TBW:(ci + 1) * TBW],
                                start=(idx == 0), stop=(idx == CI - 1)),
                                inc=1 if lastm else 0)
                    tok_mmD[(tb, jq)] = tok_mm
                    for jc in jcs:
                        eb = jc % 4
                        t_xl = pool.dma([tok_o_prev.get(eb)],
                                        lambda e, tb=tb, jc=jc, eb=eb: e.dma_start(
                                            out=xt_d[eb][:, :],
                                            in_=xT[jc * 128:(jc + 1) * 128,
                                                   tb * TBW:(tb + 1) * TBW]))
                        t_g = act.op([tok_mm, tok_sg_prev.get(eb), t_zc,
                                      t_s2[jc]],
                                     lambda e, psrc=psrc, pcol=pcol, jq=jq, jc=jc,
                                     eb=eb: e.activation(
                                         g_scr[eb][:, :],
                                         psrc[:, pcol + (jc - 2 * jq) * TBW:
                                              pcol + (jc - 2 * jq + 1) * TBW],
                                         AF.Silu, bias=zero_col[:, 0:1],
                                         scale=scl[:, 22 + jc:23 + jc]))
                        tok_gD[(tb, jc)] = t_g
                        t_sg = dve.op([t_g, tok_o_prev.get(eb)],
                                      lambda e, jc=jc, tb=tb, eb=eb: e.tensor_mul(
                                          sg_d[eb][:, :], g_scr[eb][:, :],
                                          s_sb[jc][:, tb * TBW:(tb + 1) * TBW]))
                        tok_sg_prev[eb] = t_sg
                        o_eng = dve if tb == NTB - 1 else pool
                        t_o = o_eng.op([t_sg, t_xl, tok_od_prev.get(eb)],
                                       lambda e, eb=eb: e.tensor_add(
                                           o_scr[eb][:, :], sg_d[eb][:, :],
                                           xt_d[eb][:, :]))
                        tok_o_prev[eb] = t_o
                        odq = act if jc % 2 == 0 else syn
                        t_od = odq.dma([t_o], lambda e, tb=tb, jc=jc, eb=eb: e.dma_start(
                            out=outT[jc * 128:(jc + 1) * 128, tb * TBW:(tb + 1) * TBW],
                            in_=o_scr[eb][:, :]))
                        tok_oD[(tb, jc)] = t_od
                        tok_od_prev[eb] = t_od

            syn.waits([tok_oD[(tb, jc)] for tb in range(NTB) for jc in range(DC)])
            flush("ALL")

    return nc


_NC_CACHE = None


def kernel(L0_attn_x, L0_attn_Wq, L0_attn_Wv, L0_attn_Wg, L0_attn_Wo,
           L0_attn_beta, L0_attn_ln_gamma, L0_attn_ln_beta,
           L0_ffn_clip_x, L0_ffn_clip_down,
           L0_ffn_w_gate, L0_ffn_w_up, L0_ffn_w_down):
    global _NC_CACHE
    x = np.asarray(L0_attn_x, np.float32)
    wq = np.asarray(L0_attn_Wq, np.float32)
    wv = np.asarray(L0_attn_Wv, np.float32)
    wg = np.asarray(L0_attn_Wg, np.float32)
    wo = np.asarray(L0_attn_Wo, np.float32)
    beta = np.asarray(L0_attn_beta, np.float32)
    gam = np.asarray(L0_attn_ln_gamma, np.float32)
    bln = np.asarray(L0_attn_ln_beta, np.float32)
    cx = np.asarray(L0_ffn_clip_x, np.float32)
    cd = np.asarray(L0_ffn_clip_down, np.float32)
    wgate = np.ascontiguousarray(np.asarray(L0_ffn_w_gate, np.float32))
    wup = np.ascontiguousarray(np.asarray(L0_ffn_w_up, np.float32))
    wdown = np.ascontiguousarray(np.asarray(L0_ffn_w_down, np.float32))

    wqT = np.ascontiguousarray(wq.T)
    wvT = np.ascontiguousarray(wv.T)
    wgT = np.ascontiguousarray(wg.T)
    woT = np.ascontiguousarray(wo.T)
    beta_pc = np.ascontiguousarray(np.repeat(beta.reshape(DC, 2), 64, axis=1).T)
    gam_pc = np.ascontiguousarray(gam.reshape(DC, 128).T)
    bln_pc = np.ascontiguousarray(bln.reshape(DC, 128).T)
    clip2 = np.ascontiguousarray(np.repeat(
        np.concatenate([cx.reshape(1), cd.reshape(1)]).reshape(1, 2), 128, axis=0))

    if _NC_CACHE is None:
        _NC_CACHE = build_nc()
    nc = _NC_CACHE

    in_maps = []
    for b in range(NCORES):
        in_maps.append({
            "xT": np.ascontiguousarray(x[b].T),
            "wqT": wqT, "wvT": wvT, "wgT": wgT, "woT": woT,
            "wg_s": wgate, "wu_s": wup, "wd_s": wdown,
            "beta_pc": beta_pc, "gam_pc": gam_pc, "bln_pc": bln_pc,
            "clip2": clip2,
        })
    res = run_bass_kernel_spmd(nc, in_maps, core_ids=list(range(NCORES)))
    out = np.stack([np.asarray(res.results[b]["outT"]).T for b in range(NCORES)], axis=0)
    return np.ascontiguousarray(out.astype(np.float32))


# version 41
# speedup vs baseline: 1.0064x; 1.0064x over previous
import contextlib
import numpy as np

import bass_rust
import concourse.bass as bass
import concourse.mybir as mybir
from concourse.bass_utils import run_bass_kernel_spmd

F32 = mybir.dt.float32
BF16 = mybir.dt.bfloat16
I32 = mybir.dt.int32
AF = mybir.ActivationFunctionType
ALU = mybir.AluOpType
AXX = mybir.AxisListType.X
AXC = mybir.AxisListType.C
RO_ABSMAX = bass_rust.ReduceOp.absmax

T = 2048
D = 1024
DC = 8
II = 2816
CI = 22
TBW = 512
NTB = 4
NCORES = 8
LN_EPS = 1e-5
R127 = float(np.float32(1.0) / np.float32(127.0))
CLIP_LO = -128.49
CLIP_HI = 127.49
RND_MAGIC = 12582912.0
ARENA_W = 52400
PE_FULL_TAGS = frozenset()


class Eng:
    def __init__(self, name, blockname, sem, dma_sems=()):
        self.name = name
        self.blockname = blockname
        self.sem = sem
        self.n = 0
        self.seen = {}
        self.items = []
        self.dch = [[s, 0] for s in dma_sems]
        self.rr = 0

    def _prep(self, deps):
        ws = []
        for d in deps:
            if d is None:
                continue
            sem, val = d
            k = id(sem)
            if self.seen.get(k, 0) >= val:
                continue
            self.seen[k] = val
            ws.append((sem, val))
        return ws

    def op(self, deps, fn, inc=1):
        ws = self._prep(deps)
        tok = None
        if inc:
            self.n += inc
            tok = (self.sem, self.n)
        sem = self.sem

        def item(e, ws=ws, fn=fn, inc=inc, sem=sem):
            for (s, v) in ws:
                e.wait_ge(s, v)
            inst = fn(e)
            if inc:
                inst.then_inc(sem, inc)

        self.items.append(item)
        return tok

    def dma(self, deps, fn, inc=16):
        ch = self.rr % len(self.dch)
        self.rr += 1
        entry = self.dch[ch]
        sem_c, prev = entry[0], entry[1]
        deps = list(deps)
        if prev:
            deps.append((sem_c, prev))
        ws = self._prep(deps)
        entry[1] = prev + inc
        tok = (sem_c, entry[1])

        def item(e, ws=ws, fn=fn, sem_c=sem_c, inc=inc):
            for (s, v) in ws:
                e.wait_ge(s, v)
            fn(e).then_inc(sem_c, inc)

        self.items.append(item)
        return tok

    def waits(self, deps):
        ws = self._prep(deps)
        if not ws:
            return

        def item(e, ws=ws):
            for (s, v) in ws:
                e.wait_ge(s, v)

        self.items.append(item)


class TV:

    def __init__(self, h, c0, ncols, rows):
        self.h = h
        self.c0 = c0
        self.ncols = ncols
        self.rows = rows

    def __getitem__(self, key):
        ps, cs = key
        if self.rows == 1:
            ps = slice(0, 1)
        start = cs.start if cs.start is not None else 0
        stop = cs.stop if cs.stop is not None else self.ncols
        return self.h[ps, self.c0 + start:self.c0 + stop]

    def rr(self, spec, **kw):
        return self.h[:, self.c0:self.c0 + self.ncols].rearrange(spec, **kw)


class Arena:
    def __init__(self, h32):
        self.h = {F32: h32, BF16: h32.bitcast(BF16), I32: h32.bitcast(I32)}
        self.mult = {F32: 1, BF16: 2, I32: 1}
        self.free = [[0, ARENA_W]]
        self.live = {}

    def alloc(self, name, elems, dt, rows=128):
        m = self.mult[dt]
        words = -(-elems // m)
        words = -(-words // 16) * 16
        for i, (off, n) in enumerate(self.free):
            if n >= words:
                if n == words:
                    self.free.pop(i)
                else:
                    self.free[i] = [off + words, n - words]
                self.live[name] = (off, words)
                return TV(self.h[dt], off * m, elems, rows)
        raise RuntimeError(f"arena OOM for {name}: need {words}w, free={self.free}")

    def view(self, name, elems, dt, rows=128, word_off=0):
        off, words = self.live[name]
        m = self.mult[dt]
        return TV(self.h[dt], (off + word_off) * m, elems, rows)

    def release(self, name):
        off, words = self.live.pop(name)
        self.free.append([off, words])
        self.free.sort()
        merged = []
        for seg in self.free:
            if merged and merged[-1][0] + merged[-1][1] == seg[0]:
                merged[-1][1] += seg[1]
            else:
                merged.append(list(seg))
        self.free = merged


def build_nc():
    nc = bass.Bass("TRN2", target_bir_lowering=False, debug=False)

    xT = nc.dram_tensor("xT", [D, T], F32, kind="ExternalInput").ap()
    wqT = nc.dram_tensor("wqT", [D, D], F32, kind="ExternalInput").ap()
    wvT = nc.dram_tensor("wvT", [D, D], F32, kind="ExternalInput").ap()
    wgT = nc.dram_tensor("wgT", [D, D], F32, kind="ExternalInput").ap()
    woT = nc.dram_tensor("woT", [D, D], F32, kind="ExternalInput").ap()
    wg_s = nc.dram_tensor("wg_s", [II, D], F32, kind="ExternalInput").ap()
    wu_s = nc.dram_tensor("wu_s", [II, D], F32, kind="ExternalInput").ap()
    wd_s = nc.dram_tensor("wd_s", [D, II], F32, kind="ExternalInput").ap()
    beta_pc = nc.dram_tensor("beta_pc", [128, DC], F32, kind="ExternalInput").ap()
    gam_pc = nc.dram_tensor("gam_pc", [128, DC], F32, kind="ExternalInput").ap()
    bln_pc = nc.dram_tensor("bln_pc", [128, DC], F32, kind="ExternalInput").ap()
    clip2 = nc.dram_tensor("clip2", [128, 2], F32, kind="ExternalInput").ap()

    outT = nc.dram_tensor("outT", [D, T], F32, kind="ExternalOutput").ap()

    h_dram = nc.dram_tensor("h_dram", [II, T], F32).ap()
    xn_dram = nc.dram_tensor("xn_dram", [D, T], F32).ap()
    rows_bounce = nc.dram_tensor("rows_bounce", [4, T], F32).ap()
    bx_dram = nc.dram_tensor("bx_dram", [128, 1], F32).ap()
    bh_dram = nc.dram_tensor("bh_dram", [128, 1], F32).ap()
    ccx_in = nc.dram_tensor("ccx_in", [1, 1], F32).ap()
    ccx_out = nc.dram_tensor("ccx_out", [1, 1], F32, addr_space="Shared").ap()
    cch_in = nc.dram_tensor("cch_in", [1, 1], F32).ap()
    cch_out = nc.dram_tensor("cch_out", [1, 1], F32, addr_space="Shared").ap()

    es = contextlib.ExitStack()
    with es:
        sems = {n: es.enter_context(nc.semaphore("sem_" + n))
                for n in ["pe", "act", "dve", "pool", "syn"]}
        dsem = {q: [es.enter_context(nc.semaphore(f"dma_{q}{i}")) for i in range(k)]
                for q, k in [("syn", 6), ("act", 4), ("pool", 3)]}
        pe = Eng("pe", "tensor", sems["pe"])
        act = Eng("act", "scalar", sems["act"], dsem["act"])
        dve = Eng("dve", "vector", sems["dve"])
        pool = Eng("pool", "gpsimd", sems["pool"], dsem["pool"])
        syn = Eng("syn", "sync", sems["syn"], dsem["syn"])
        engines = [pe, act, dve, pool, syn]

        def flush(tag):
            used = [e for e in engines if e.items]
            if not used:
                return
            with nc.Block() as blk:
                for e in used:
                    items = e.items
                    e.items = []
                    getattr(blk, e.blockname)(
                        lambda eng, items=items: [it(eng) for it in items])

        def softbar(tag=None):
            toks = [(e.sem, e.n) for e in engines]
            for e in engines:
                toks += [(c[0], c[1]) for c in e.dch if c[1]]
            for e in engines:
                if e is pe and tag not in PE_FULL_TAGS:
                    continue
                e.waits([t for t in toks if t[0] is not e.sem])

        def SB(name, shape, dt):
            return es.enter_context(nc.sbuf_tensor(name, shape, dt))

        ones_f32r = SB("ones_f32r", [1, 128], F32)
        ones_bf16c = SB("ones_bf16c", [128, 1], BF16)
        ones_bf16r = SB("ones_bf16r", [1, 128], BF16)
        zero_col = SB("zero_col", [128, 1], F32)
        eps_col = SB("eps_col", [128, 1], F32)
        clip_sb = SB("clip_sb", [128, 2], F32)
        sig_sb = SB("sig_sb", [128, 2], F32)
        betab = SB("betab", [128, DC], F32)
        lamcols = SB("lamcols", [128, DC], F32)
        gcols = SB("gcols", [128, DC], F32)
        bcols = SB("bcols", [128, DC], F32)
        scl = SB("scl", [128, 64], F32)
        g11 = SB("g11", [1, 4], F32)
        gm_row = SB("gm_row", [1, 128], F32)

        arena_h = es.enter_context(nc.sbuf_tensor("arena", [128, ARENA_W], F32))
        ar = Arena(arena_h)
        psumA = es.enter_context(nc.psum_tensor("psumA", [128, T], F32))
        psumB = es.enter_context(nc.psum_tensor("psumB", [128, T], F32))

        ones_big = ar.alloc("ones_big", T, F32)
        wstage = [ar.alloc(f"wstg{i}", D, F32) for i in range(2)]

        t_c2 = syn.dma([], lambda e: e.dma_start(out=clip_sb[:, :], in_=clip2))
        t_bp = syn.dma([], lambda e: e.dma_start(out=betab[:, :], in_=beta_pc))
        t_gp = syn.dma([], lambda e: e.dma_start(out=gcols[:, :], in_=gam_pc))
        t_bl = syn.dma([], lambda e: e.dma_start(out=bcols[:, :], in_=bln_pc))
        t_ones = dve.op([], lambda e: e.memset(ones_big[:, :], 1.0))
        t_o32 = dve.op([], lambda e: e.memset(ones_f32r[:, :], 1.0))
        t_obc = dve.op([], lambda e: e.memset(ones_bf16c[:, :], 1.0))
        t_obr = dve.op([], lambda e: e.memset(ones_bf16r[:, :], 1.0))
        t_zc = dve.op([], lambda e: e.memset(zero_col[:, :], 0.0))
        t_ec = dve.op([], lambda e: e.memset(eps_col[:, :], LN_EPS))
        t_sig = act.op([t_c2], lambda e: e.activation(sig_sb[:, :], clip_sb[:, :], AF.Sigmoid))
        t_lam = act.op([t_bp], lambda e: e.activation(lamcols[:, :], betab[:, :], AF.Sigmoid))

        xnb = [ar.alloc(f"xnb{d}", T, BF16) for d in range(DC)]
        wv_sb = [ar.alloc(f"wv{d}", D, BF16) for d in range(DC)]
        wq_sb = [ar.alloc(f"wq{d}", D, BF16) for d in range(DC)]

        t_wload = {}
        t_wstage_rd = {}
        wsi = 0

        def emit_wpair(d):
            nonlocal wsi
            for wname, src, dst, ceng in [("wv", wvT, wv_sb, act),
                                          ("wq", wqT, wq_sb, dve)]:
                b = wsi % 2
                tl = pool.dma([t_wstage_rd.get(wsi - 2)],
                              lambda e, s=src, d=d, b=b: e.dma_start(
                                  out=wstage[b][:, :], in_=s[d * 128:(d + 1) * 128, :]))
                if ceng is act:
                    tc = act.op([tl], lambda e, d=d, b=b, dst=dst: e.copy(
                        dst[d][:, :], wstage[b][:, :]))
                else:
                    tc = dve.op([tl], lambda e, d=d, b=b, dst=dst: e.tensor_copy(
                        dst[d][:, :], wstage[b][:, :]))
                t_wload[(wname, d)] = tc
                t_wstage_rd[wsi] = tc
                wsi += 1

        x_stage = [ar.alloc(f"xstg{i}", T, F32) for i in range(3)]
        M_b = ar.alloc("M_b", T, F32)
        R_b = ar.alloc("R_b", T, F32)
        scrX = [ar.alloc(f"scrX{i}", T, F32) for i in range(2)]
        xnX = [ar.alloc(f"xnX{i}", T, F32) for i in range(2)]
        xb2 = [ar.alloc(f"xb2_{i}", T, BF16) for i in range(2)]
        xq2 = [ar.alloc(f"xq2_{i}", T, BF16) for i in range(2)]
        rows = [ar.alloc(f"rowA{i}", T, F32, rows=1) for i in range(4)]
        rA, rB, rC, rD = rows

        t_x1 = {}
        t_mmA = {}
        if True:
            ps_sx = psumA[0:1, :]
            ps_sx2 = psumB[0:1, :]
            tok_stats = None
            for d in range(DC):
                b = d % 2
                b3 = d % 3
                t_x1[d] = syn.dma([t_mmA.get(d - 3)], lambda e, d=d, b3=b3: e.dma_start(
                    out=x_stage[b3][:, :], in_=xT[d * 128:(d + 1) * 128, :]))
                tcb = act.op([t_x1[d], t_mmA.get(d - 2)], lambda e, b=b, b3=b3: e.copy(
                    xb2[b][:, :], x_stage[b3][:, :]))
                tcs = dve.op([t_x1[d], t_mmA.get(d - 2)],
                             lambda e, b=b, b3=b3: e.tensor_mul(
                                 xq2[b][:, :], x_stage[b3][:, :], x_stage[b3][:, :]))
                emit_wpair(d)
                pe.waits([tcb, tcs, t_obc])
                for tb in range(NTB):
                    pe.op([], lambda e, b=b, tb=tb, d=d: e.matmul(
                        ps_sx[0:1, tb * TBW:(tb + 1) * TBW], ones_bf16c[:, :],
                        xb2[b][:, tb * TBW:(tb + 1) * TBW],
                        start=(d == 0), stop=(d == DC - 1)), inc=0)
                    last = (tb == NTB - 1)
                    tmm = pe.op([], lambda e, b=b, tb=tb, d=d: e.matmul(
                        ps_sx2[0:1, tb * TBW:(tb + 1) * TBW], ones_bf16c[:, :],
                        xq2[b][:, tb * TBW:(tb + 1) * TBW],
                        start=(d == 0), stop=(d == DC - 1)),
                        inc=1 if last else 0)
                    if last:
                        t_mmA[d] = tmm
                        tok_stats = tmm
            t_ev1 = act.op([tok_stats], lambda e: e.copy(rA[:, :], ps_sx[:, :]))
            t_ev2 = act.op([tok_stats], lambda e: e.copy(rB[:, :], ps_sx2[:, :]))
            t_mu = dve.op([t_ev1], lambda e: e.tensor_scalar(
                rC[:, :], rA[:, :], 1.0 / D, None, ALU.mult))
            t_e2 = dve.op([t_ev2, t_mu], lambda e: e.tensor_scalar(
                rA[:, :], rB[:, :], 1.0 / D, None, ALU.mult))
            t_m2 = act.op([t_mu, t_ev2, t_e2], lambda e: e.activation(
                rB[:, :], rC[:, :], AF.Square))
            t_var = dve.op([t_e2, t_m2], lambda e: e.tensor_sub(
                rD[:, :], rA[:, :], rB[:, :]))
            t_sd = act.op([t_var, t_ec, t_e2], lambda e: e.activation(
                rA[:, :], rD[:, :], AF.Sqrt, bias=eps_col[0:1, 0:1], scale=1.0))
            t_rstd = dve.op([t_sd, t_var], lambda e: e.reciprocal(rD[:, :], rA[:, :]))

        if True:
            def bc128(row_ap):
                return bass.AP(row_ap.tensor, row_ap.offset,
                               [[0, 128]] + [list(dd) for dd in row_ap.ap[1:]])

            t_sp1 = act.dma([t_mu], lambda e: e.dma_start(
                out=rows_bounce[0:1, :], in_=rC[:, :]))
            t_sp2 = syn.dma([t_rstd], lambda e: e.dma_start(
                out=rows_bounce[1:2, :], in_=rD[:, :]))
            t_Mb = act.dma([t_sp1], lambda e: e.dma_start(
                out=M_b[:, :], in_=bc128(rows_bounce[0:1, :])))
            t_Rb = syn.dma([t_sp2], lambda e: e.dma_start(
                out=R_b[:, :], in_=bc128(rows_bounce[1:2, :])))

            gslot = 0
            t_gm = None
            prev = {}
            for d in range(DC):
                b = d % 2
                b3 = d % 3
                t_x2 = syn.dma([prev.get(("sub", d - 3)), tok_stats],
                               lambda e, d=d, b3=b3: e.dma_start(
                                   out=x_stage[b3][:, :],
                                   in_=xT[d * 128:(d + 1) * 128, :]))
                s2eng = pool if d in (2, 5) else dve
                t_sub = s2eng.op([t_x2, t_Mb, prev.get(("mul", d - 2))],
                                 lambda e, b=b, b3=b3: e.tensor_sub(
                                     scrX[b][:, :], x_stage[b3][:, :], M_b[:, :]))
                prev[("sub", d)] = t_sub
                t_xn = s2eng.op([t_sub, t_Rb, prev.get(("cast", d - 2)),
                                 prev.get(("rm", d - 2)), prev.get(("spill", d - 2))],
                                lambda e, b=b: e.tensor_mul(
                                    xnX[b][:, :], scrX[b][:, :], R_b[:, :]))
                prev[("mul", d)] = t_xn
                prev[("cast", d)] = act.op([t_xn], lambda e, d=d, b=b: e.copy(
                    xnb[d][:, :], xnX[b][:, :]))
                rm = dve.op([t_xn], lambda e, d=d, b=b: e.tensor_reduce(
                    scl[:, 2 + b:3 + b], xnX[b][:, :], AXX, ALU.max,
                    apply_absolute_value=True))
                prev[("rm", d)] = rm
                prev[("spill", d)] = act.dma([t_xn], lambda e, d=d, b=b: e.dma_start(
                    out=xn_dram[d * 128:(d + 1) * 128, :], in_=xnX[b][:, :]))
                if d == 0:
                    t_gm = dve.op([rm], lambda e: e.tensor_copy(scl[:, 0:1], scl[:, 2:3]))
                else:
                    ns = 1 - gslot
                    t_gm = dve.op([rm, t_gm],
                                  lambda e, b=b, g=gslot, ns=ns: e.tensor_tensor(
                                      scl[:, ns:ns + 1], scl[:, g:g + 1],
                                      scl[:, 2 + b:3 + b], ALU.max))
                    gslot = ns
            t_xnb_last = prev[("cast", DC - 1)]
        for nm in (["xstg0", "xstg1", "xstg2", "M_b", "R_b", "scrX0", "scrX1", "xnX0",
                    "xnX1", "xb2_0", "xb2_1", "xq2_0", "xq2_1",
                    "rowA0", "rowA1", "rowA2", "rowA3"]):
            ar.release(nm)

        softbar("C1")
        i32scr = ar.alloc("i32scr", T, I32)
        kx = [ar.alloc(f"kx{d}", T, BF16) for d in range(DC)]
        qscr = ar.alloc("qscr", T, F32)
        xn_stage = [ar.alloc(f"xnstg{i}", T, F32) for i in range(2)]
        lamb = ar.alloc("lamb", T, F32)
        v_sb = ar.alloc("v_sb", T, F32)
        states = ar.alloc("states", T, F32)
        yb = [ar.alloc(f"yb{d}", T, BF16) for d in range(DC)]

        t_b1 = pool.dma([t_gm], lambda e, g=gslot: e.dma_start(
            out=bx_dram, in_=scl[:, g:g + 1]))
        t_b2 = pool.dma([t_b1], lambda e: e.dma_start(
            out=gm_row[:, :], in_=bx_dram.rearrange("a b -> b a")))
        t_gx = dve.op([t_b2], lambda e: e.tensor_reduce(
            g11[:, 0:1], gm_row[:, :], AXX, ALU.max))
        t_cx1 = pool.dma([t_gx], lambda e: e.dma_start(out=ccx_in, in_=g11[:, 0:1]))
        t_cc1 = pool.dma([t_cx1], lambda e: e.collective_compute(
            "AllReduce", ALU.max, replica_groups=[list(range(NCORES))],
            ins=[ccx_in.opt()], outs=[ccx_out.opt()]), inc=1)
        t_cx2 = pool.dma([t_cc1], lambda e: e.dma_start(
            out=g11[:, 0:1], in_=ccx_out))

        t_grow = pool.op([t_cx2, t_o32], lambda e: e.tensor_single_scalar(
            gm_row[:, :], ones_f32r[:, :], g11[0:1, 0:1], ALU.mult))
        t_b3 = pool.dma([t_grow], lambda e: e.dma_start(
            out=bx_dram.rearrange("a b -> b a"), in_=gm_row[:, :]))
        t_b4 = pool.dma([t_b3], lambda e: e.dma_start(
            out=scl[:, 6:7], in_=bx_dram))
        t_t1c = pool.op([t_b4, t_sig], lambda e: e.tensor_mul(
            scl[:, 7:8], scl[:, 6:7], sig_sb[:, 0:1]))
        t_sxc = pool.op([t_t1c], lambda e: e.tensor_scalar(
            scl[:, 5:6], scl[:, 7:8], R127, None, ALU.mult))

        if True:
            v_ps = psumA[:, :]
            q_ps = psumB[:, :]
            t_scan = None
            t_vev = None
            t_yTT = None
            t_yTTs = {}
            t_rxc = None
            prevq = {}
            for c in range(DC):
                t_lamb = act.op([t_lam, t_ones, t_zc, t_scan],
                                lambda e, c=c: e.activation(
                                    lamb[:, :], ones_big[:, :], AF.Copy,
                                    bias=0.0, scale=lamcols[:, c:c + 1]))
                pe.waits([t_scan, t_Rb, t_xnb_last])
                tok_v = None
                for d in range(DC):
                    for tb in range(NTB):
                        lastv = (d == DC - 1 and tb == NTB - 1)
                        tok_v = pe.op([], lambda e, c=c, d=d, tb=tb: e.matmul(
                            v_ps[:, tb * TBW:(tb + 1) * TBW],
                            wv_sb[d][:, c * 128:(c + 1) * 128],
                            xnb[d][:, tb * TBW:(tb + 1) * TBW],
                            start=(d == 0), stop=(d == DC - 1)),
                            inc=1 if lastv else 0)
                t_vev = tok_v
                t_scan = dve.op([tok_v, t_lamb, t_yTT],
                                lambda e: e.tensor_tensor_scan(
                                    states[:, :], lamb[:, :], v_ps[:, :], 0.0,
                                    ALU.mult, ALU.add))
                pe.waits([t_yTT])
                tok_q = None
                for d in range(DC):
                    for tb in range(NTB):
                        lastq = (d == DC - 1 and tb == NTB - 1)
                        tok_q = pe.op([], lambda e, c=c, d=d, tb=tb: e.matmul(
                            q_ps[:, tb * TBW:(tb + 1) * TBW],
                            wq_sb[d][:, c * 128:(c + 1) * 128],
                            xnb[d][:, tb * TBW:(tb + 1) * TBW],
                            start=(d == 0), stop=(d == DC - 1)),
                            inc=1 if lastq else 0)
                t_yTT = dve.op([tok_q, t_scan], lambda e, c=c: e.tensor_mul(
                    yb[c][:, :], q_ps[:, :], states[:, :]))
                t_yTTs[c] = t_yTT

                if c >= 4:
                    if c == 4:
                        t_rxc = dve.op([t_sxc], lambda e: e.reciprocal(
                            scl[:, 4:5], scl[:, 5:6]))
                    for d in (2 * (c - 4), 2 * (c - 4) + 1):
                        bq = d % 2
                        t_xl = syn.dma([prevq.get(("ts1", d - 2))],
                                       lambda e, d=d, bq=bq: e.dma_start(
                                           out=xn_stage[bq][:, :],
                                           in_=xn_dram[d * 128:(d + 1) * 128, :]))
                        t_q1 = dve.op([t_xl, t_rxc, prevq.get(("ts2", d - 1))],
                                      lambda e, bq=bq: e.tensor_scalar(
                                          qscr[:, :], xn_stage[bq][:, :],
                                          scl[:, 4:5], CLIP_HI, ALU.mult, ALU.min))
                        prevq[("ts1", d)] = t_q1
                        t_q2 = dve.op([t_q1, prevq.get(("cp", d - 1))],
                                      lambda e: e.tensor_scalar(
                                          i32scr[:, :], qscr[:, :], CLIP_LO,
                                          None, ALU.max))
                        prevq[("ts2", d)] = t_q2
                        prevq[("cp", d)] = dve.op([t_q2], lambda e, d=d: e.tensor_copy(
                            kx[d][:, :], i32scr[:, :]))
            wg_sb = [ar.view(f"wv{d}", D, BF16) for d in range(DC)]
            for d in range(DC):
                b = wsi % 2
                tl = syn.dma([t_wstage_rd.get(wsi - 2), tok_v],
                             lambda e, d=d, b=b: e.dma_start(
                                 out=wstage[b][:, :],
                                 in_=wgT[d * 128:(d + 1) * 128, :]))
                tc = act.op([tl], lambda e, d=d, b=b: e.copy(
                    wg_sb[d][:, :], wstage[b][:, :]))
                t_wload[("wg", d)] = tc
                t_wstage_rd[wsi] = tc
                wsi += 1
        for nm in (["lamb", "v_sb", "states", "qscr", "xnstg0", "xnstg1"] +
                   [f"wq{d}" for d in range(DC)]):
            ar.release(nm)

        softbar("C2")
        NM2 = ar.alloc("NM2", T, F32)
        R2 = ar.alloc("R2", T, F32)
        ysq = [ar.alloc(f"ysq{i}", T, BF16) for i in range(2)]
        rows = [ar.alloc(f"rw{i}", T, F32, rows=1) for i in range(4)]
        rA, rB, rC, rD = rows

        if True:
            ps_sy = psumA[0:1, :]
            ps_sy2 = psumA[32:33, :]
            t_ysq_war = {}
            tok_sy = None
            for d in range(DC):
                yq = d % 2
                t_ysq = act.op([t_yTTs[d], t_ysq_war.get(d - 2)],
                               lambda e, d=d, yq=yq: e.activation(
                                   ysq[yq][:, :], yb[d][:, :], AF.Square))
                pe.waits([t_ysq, t_obc, t_scan])
                for tb in range(NTB):
                    pe.op([], lambda e, d=d, tb=tb: e.matmul(
                        ps_sy[0:1, tb * TBW:(tb + 1) * TBW], ones_bf16c[:, :],
                        yb[d][:, tb * TBW:(tb + 1) * TBW],
                        start=(d == 0), stop=(d == DC - 1)), inc=0)
                    last = (tb == NTB - 1)
                    tok_sy = pe.op([], lambda e, d=d, tb=tb, yq=yq: e.matmul(
                        ps_sy2[0:1, tb * TBW:(tb + 1) * TBW], ones_bf16c[:, :],
                        ysq[yq][:, tb * TBW:(tb + 1) * TBW],
                        start=(d == 0), stop=(d == DC - 1)),
                        inc=1 if last else 0)
                t_ysq_war[d] = tok_sy

            t_ev1 = act.op([tok_sy], lambda e: e.copy(rA[:, :], ps_sy[:, :]))
            t_ev2 = act.op([tok_sy], lambda e: e.copy(rB[:, :], ps_sy2[:, :]))
            t_mu = dve.op([t_ev1], lambda e: e.tensor_scalar(
                rC[:, :], rA[:, :], 1.0 / D, None, ALU.mult))
            t_e2 = dve.op([t_ev2, t_mu], lambda e: e.tensor_scalar(
                rA[:, :], rB[:, :], 1.0 / D, None, ALU.mult))
            t_m2 = act.op([t_mu, t_ev2, t_e2], lambda e: e.activation(
                rB[:, :], rC[:, :], AF.Square))
            t_var = dve.op([t_e2, t_m2], lambda e: e.tensor_sub(
                rD[:, :], rA[:, :], rB[:, :]))
            t_sd = act.op([t_var, t_e2], lambda e: e.activation(
                rA[:, :], rD[:, :], AF.Sqrt, bias=eps_col[0:1, 0:1], scale=1.0))
            t_rstd = dve.op([t_sd, t_var], lambda e: e.reciprocal(rD[:, :], rA[:, :]))
            t_n1 = dve.op([t_rstd, t_sd], lambda e: e.tensor_mul(
                rA[:, :], rC[:, :], rD[:, :]))
            t_n2 = dve.op([t_n1, t_m2], lambda e: e.tensor_scalar(
                rB[:, :], rA[:, :], -1.0, None, ALU.mult))

        if True:
            def bc128(row_ap):
                return bass.AP(row_ap.tensor, row_ap.offset,
                               [[0, 128]] + [list(dd) for dd in row_ap.ap[1:]])

            t_sp3 = act.dma([t_rstd], lambda e: e.dma_start(
                out=rows_bounce[2:3, :], in_=rD[:, :]))
            t_sp4 = syn.dma([t_n2], lambda e: e.dma_start(
                out=rows_bounce[3:4, :], in_=rB[:, :]))
            t_R2 = act.dma([t_sp3], lambda e: e.dma_start(
                out=R2[:, :], in_=bc128(rows_bounce[2:3, :])))
            t_NM2 = syn.dma([t_sp4], lambda e: e.dma_start(
                out=NM2[:, :], in_=bc128(rows_bounce[3:4, :])))
        for nm in ["ysq0", "ysq1", "rw0", "rw1", "rw2", "rw3"]:
            ar.release(nm)

        softbar("C3")
        ga_sb = [ar.alloc(f"ga{i}", T, BF16) for i in range(2)]
        tscr1p = [ar.alloc(f"tscr1{i}", T, BF16) for i in range(2)]
        tscr2p = [ar.alloc(f"tscr2{i}", T, BF16) for i in range(2)]
        if True:
            t_sil = {}
            t_yg = {}
            tt3_war = {}
            for c in range(DC):
                b = c % 2
                ga_ps = psumA[:, :] if b == 0 else psumB[:, :]
                pe.waits([t_sil.get(c - 2), t_ev1, t_ev2, t_yTT])
                tok_g = None
                for d in range(DC):
                    for tb in range(NTB):
                        lastg = (d == DC - 1 and tb == NTB - 1)
                        tok_g = pe.op([t_wload[("wg", d)]],
                                      lambda e, c=c, d=d, tb=tb, ga_ps=ga_ps: e.matmul(
                                          ga_ps[:, tb * TBW:(tb + 1) * TBW],
                                          wg_sb[d][:, c * 128:(c + 1) * 128],
                                          xnb[d][:, tb * TBW:(tb + 1) * TBW],
                                          start=(d == 0), stop=(d == DC - 1)),
                                      inc=1 if lastg else 0)
                t_sil[c] = act.op([tok_g, t_yg.get(c - 2)],
                                  lambda e, b=b, ga_ps=ga_ps: e.activation(
                                      ga_sb[b][:, :], ga_ps[:, :], AF.Silu))
                tt1 = dve.op([t_R2, t_yg.get(c - 2)], lambda e, c=c, b=b: e.tensor_mul(
                    tscr1p[b][:, :], yb[c][:, :], R2[:, :]))
                tt2 = dve.op([tt1, t_NM2, tt3_war.get(c - 2)],
                             lambda e, b=b: e.tensor_add(
                                 tscr2p[b][:, :], tscr1p[b][:, :], NM2[:, :]))
                tt3 = dve.op([tt2, t_gp, t_bl, t_yg.get(c - 2)],
                             lambda e, c=c, b=b: e.tensor_scalar(
                                 tscr1p[b][:, :], tscr2p[b][:, :], gcols[:, c:c + 1],
                                 bcols[:, c:c + 1], ALU.mult, ALU.add))
                tt3_war[c] = tt3
                t_yg[c] = dve.op([tt3, t_sil[c]], lambda e, c=c, b=b: e.tensor_mul(
                    yb[c][:, :], tscr1p[b][:, :], ga_sb[b][:, :]))
        for nm in (["ga0", "ga1", "tscr10", "tscr11", "tscr20", "tscr21", "NM2", "R2"] +
                   [f"wv{d}" for d in range(DC)] + [f"xnb{d}" for d in range(DC)]):
            ar.release(nm)

        softbar("B")
        wo_sb = [ar.alloc(f"wo{d}", D, BF16) for d in range(DC)]
        NPB = 3
        ws_g = [ar.alloc(f"wsg{i}", D, F32) for i in range(NPB)]
        ws_u = [ar.alloc(f"wsu{i}", D, F32) for i in range(NPB)]
        ktm_g = [ar.alloc(f"ktmg{i}", D, BF16) for i in range(NPB)]
        ktm_u = [ar.alloc(f"ktmu{i}", D, BF16) for i in range(NPB)]
        kwT_g = [ar.alloc(f"kwTg{i}", DC * 128, BF16) for i in range(NPB)]
        kwT_u = [ar.alloc(f"kwTu{i}", DC * 128, BF16) for i in range(NPB)]
        h_tile = [ar.alloc(f"htile{i}", T, F32) for i in range(2)]
        sg_scr = [ar.alloc(f"sgscr{i}", TBW, F32) for i in range(2)]
        up_scr = [ar.alloc(f"upscr{i}", TBW, F32) for i in range(2)]

        if True:
            ps_g = psumA[:, :]
            ps_u = psumB[:, :]
            tok_c3_sil = (act.sem, act.n)
            for d in range(DC):
                b = wsi % 2
                tl = syn.dma([t_wstage_rd.get(wsi - 2)],
                             lambda e, d=d, b=b: e.dma_start(
                                 out=wstage[b][:, :], in_=woT[d * 128:(d + 1) * 128, :]))
                tc = act.op([tl], lambda e, d=d, b=b: e.copy(
                    wo_sb[d][:, :], wstage[b][:, :]))
                t_wload[("wo", d)] = tc
                t_wstage_rd[wsi] = tc
                wsi += 1

            tok_wread = {}
            tok_k = {}
            tok_scg = {}
            tok_tp = {}
            tok_epi = {}
            tok_hdma = [None, None]
            tok_hread = [None, None]
            state = {"i32rd": None}
            t_hm = None
            hslot = 0

            def prep_b(ci):
                b = ci % NPB
                tlg = syn.dma([tok_wread.get(("g", ci - NPB))],
                              lambda e, ci=ci, b=b: e.dma_start(
                                  out=ws_g[b][:, :], in_=wg_s[ci * 128:(ci + 1) * 128, :]))
                tlu = syn.dma([tok_wread.get(("u", ci - NPB))],
                              lambda e, ci=ci, b=b: e.dma_start(
                                  out=ws_u[b][:, :], in_=wu_s[ci * 128:(ci + 1) * 128, :]))
                for w, tl, ws_, ktm_, kwT_, sm0, cg0, rw0 in [
                        ("g", tlg, ws_g, ktm_g, kwT_g, 32, 38, 44),
                        ("u", tlu, ws_u, ktm_u, kwT_u, 35, 41, 47)]:
                    sm = sm0 + b
                    cg = cg0 + b
                    rw = rw0 + b
                    t_red = dve.op([tl], lambda e, b=b, ws_=ws_, sm=sm: e.tensor_reduce(
                        scl[:, sm:sm + 1], ws_[b][:, :], AXX, ALU.max,
                        apply_absolute_value=True))
                    t_sw = dve.op([t_red], lambda e, sm=sm: e.tensor_scalar(
                        scl[:, sm:sm + 1], scl[:, sm:sm + 1], R127, None, ALU.mult))
                    t_rw = dve.op([t_sw], lambda e, sm=sm, rw=rw: e.reciprocal(
                        scl[:, rw:rw + 1], scl[:, sm:sm + 1]))
                    tok_scg[(w, ci)] = dve.op(
                        [t_sw, t_sxc, tok_epi.get((w, NTB - 1, ci - NPB))],
                        lambda e, sm=sm, cg=cg: e.tensor_mul(
                            scl[:, cg:cg + 1], scl[:, sm:sm + 1], scl[:, 5:6]))
                    t_i = dve.op([t_rw, state["i32rd"]],
                                 lambda e, b=b, ws_=ws_, rw=rw: e.tensor_scalar(
                                     i32scr[:, 0:D], ws_[b][:, :],
                                     scl[:, rw:rw + 1], None, ALU.mult))
                    tok_wread[(w, ci)] = t_i
                    t_k = dve.op([t_i, tok_tp.get((w, ci - NPB))],
                                 lambda e, b=b, ktm_=ktm_: e.tensor_copy(
                                     ktm_[b][:, :], i32scr[:, 0:D]))
                    state["i32rd"] = t_k
                    tok_k[(w, ci)] = t_k
                    tok_tp[(w, ci)] = act.dma(
                        [t_k] + [tok_epi.get((w, tb, ci - NPB)) for tb in range(NTB)],
                        lambda e, b=b, ktm_=ktm_, kwT_=kwT_: e.dma_start_transpose(
                            kwT_[b].rr("p (a b) -> p a b", a=DC), ktm_[b][:, :]))

            for pp in range(NPB):
                prep_b(pp)
            for ci in range(CI):
                b = ci % NPB
                b2 = ci % 2
                for half in range(2):
                    tbs = (2 * half, 2 * half + 1)
                    pe.waits([tok_tp[("g", ci)], tok_tp[("u", ci)], tok_c3_sil])
                    pe.waits([tok_epi.get(("g", tb, ci - 1)) for tb in tbs] +
                             [tok_epi.get(("u", tb, ci - 1)) for tb in tbs] +
                             [tok_epi.get(("p", tb, ci - 1)) for tb in tbs])
                    tok_mm = None
                    for hc in range(DC):
                        for w, kwT_, ps in [("g", kwT_g, ps_g), ("u", kwT_u, ps_u)]:
                            for tb in tbs:
                                lastm = (hc == DC - 1 and w == "u" and tb == tbs[-1])
                                tok_mm = pe.op([], lambda e, b=b, hc=hc, kwT_=kwT_, ps=ps, tb=tb: e.matmul(
                                    ps[:, tb * TBW:(tb + 1) * TBW],
                                    kwT_[b][:, hc * 128:(hc + 1) * 128],
                                    kx[hc][:, tb * TBW:(tb + 1) * TBW],
                                    start=(hc == 0), stop=(hc == DC - 1)),
                                    inc=1 if lastm else 0)
                    for tb in tbs:
                        eb = tb % 2
                        war_sg = (tok_epi.get(("p", tb - 2, ci)) or
                                  tok_epi.get(("p", tb + 2, ci - 1)))
                        t_sg = act.op([tok_mm, tok_scg[("g", ci)], t_zc, war_sg],
                                      lambda e, b=b, tb=tb, eb=eb: e.activation(
                                          sg_scr[eb][:, :],
                                          ps_g[:, tb * TBW:(tb + 1) * TBW],
                                          AF.Silu, bias=zero_col[:, 0:1],
                                          scale=scl[:, 38 + b:39 + b]))
                        t_up = act.op([tok_mm, tok_scg[("u", ci)], war_sg],
                                      lambda e, b=b, tb=tb, eb=eb: e.activation(
                                          up_scr[eb][:, :],
                                          ps_u[:, tb * TBW:(tb + 1) * TBW],
                                          AF.Copy, bias=0.0,
                                          scale=scl[:, 41 + b:42 + b]))
                        tok_epi[("g", tb, ci)] = t_sg
                        tok_epi[("u", tb, ci)] = t_up
                        tok_epi[("p", tb, ci)] = dve.op(
                            [t_sg, t_up, tok_hdma[b2], tok_hread[b2]],
                            lambda e, b2=b2, tb=tb, eb=eb: e.tensor_mul(
                                h_tile[b2][:, tb * TBW:(tb + 1) * TBW],
                                sg_scr[eb][:, :], up_scr[eb][:, :]))
                t_hfull = [tok_epi[("p", tb, ci)] for tb in range(NTB)]
                rm = dve.op(t_hfull, lambda e, b2=b2: e.tensor_reduce(
                    scl[:, 2 + b2:3 + b2], h_tile[b2][:, :], AXX, ALU.max,
                    apply_absolute_value=True))
                if ci == 0:
                    t_hm = dve.op([rm], lambda e: e.tensor_copy(scl[:, 0:1], scl[:, 2:3]))
                    hslot = 0
                else:
                    ns = 1 - hslot
                    t_hm = dve.op([rm, t_hm],
                                  lambda e, b2=b2, g=hslot, ns=ns: e.tensor_tensor(
                                      scl[:, ns:ns + 1], scl[:, g:g + 1],
                                      scl[:, 2 + b2:3 + b2], ALU.max))
                    hslot = ns
                tok_hread[b2] = rm
                if ci < CI - 2:
                    tok_hdma[b2] = pool.dma(t_hfull, lambda e, ci=ci, b2=b2: e.dma_start(
                        out=h_dram[ci * 128:(ci + 1) * 128, :], in_=h_tile[b2][:, :]))
                if ci + NPB < CI:
                    prep_b(ci + NPB)
        for nm in ([f"wsg{i}" for i in range(3)] + [f"wsu{i}" for i in range(3)] +
                   [f"ktmg{i}" for i in range(3)] + [f"ktmu{i}" for i in range(3)] +
                   [f"kwTg{i}" for i in range(3)] + [f"kwTu{i}" for i in range(3)] +
                   ["sgscr0", "sgscr1", "upscr0", "upscr1",
                    "i32scr"] + [f"kx{d}" for d in range(DC)]):
            ar.release(nm)

        softbar("B2")
        s_sb = [ar.alloc(f"ssb{d}", T, BF16) for d in range(DC)]
        kwdT = [ar.alloc(f"kwdT{j}", CI * 128, BF16) for j in range(DC)]
        wd_stage = [ar.alloc(f"wds{i}", II, F32) for i in range(2)]
        ktm_wd = [ar.alloc(f"ktmwd{i}", II, BF16) for i in range(2)]
        i32b2 = ar.alloc("i32b2", 1408, I32)
        wd32m_a = ar.alloc("wd32m_a", 1152, F32)
        wd32m_b = ar.alloc("wd32m_b", 256, F32)

        t_b1 = pool.dma([t_hm], lambda e, g=hslot: e.dma_start(
            out=bh_dram, in_=scl[:, g:g + 1]))
        t_b2 = pool.dma([t_b1], lambda e: e.dma_start(
            out=gm_row[:, :], in_=bh_dram.rearrange("a b -> b a")))
        t_gh = dve.op([t_b2], lambda e: e.tensor_reduce(
            g11[:, 0:1], gm_row[:, :], AXX, ALU.max))
        t_ch1 = pool.dma([t_gh], lambda e: e.dma_start(out=cch_in, in_=g11[:, 0:1]))
        t_cc2 = pool.dma([t_ch1], lambda e: e.collective_compute(
            "AllReduce", ALU.max, replica_groups=[list(range(NCORES))],
            ins=[cch_in.opt()], outs=[cch_out.opt()]), inc=1)
        t_ch2 = pool.dma([t_cc2], lambda e: e.dma_start(
            out=g11[:, 0:1], in_=cch_out))

        if True:
            s_ps = psumA[:, 0:2 * TBW]
            tok_b_act = (act.sem, act.n)
            t_sev = {}
            seq = 0
            for ds in range(DC):
                for tb in range(NTB):
                    half = seq % 2
                    pe.waits([t_sev.get(seq - 2), tok_b_act])
                    tok_s = None
                    for d in range(DC):
                        tok_s = pe.op([t_wload[("wo", d)], t_yg[d]],
                                      lambda e, ds=ds, d=d, tb=tb, half=half: e.matmul(
                                          s_ps[:, half * TBW:(half + 1) * TBW],
                                          wo_sb[d][:, ds * 128:(ds + 1) * 128],
                                          yb[d][:, tb * TBW:(tb + 1) * TBW],
                                          start=(d == 0), stop=(d == DC - 1)),
                                      inc=1 if d == DC - 1 else 0)
                    t_sev[seq] = act.op([tok_s], lambda e, ds=ds, tb=tb, half=half: e.copy(
                        s_sb[ds][:, tb * TBW:(tb + 1) * TBW],
                        s_ps[:, half * TBW:(half + 1) * TBW]))
                    seq += 1

            t_wdq = {}
            t_wdi = {}
            t_wdtp = {}
            t_swd = {}
            prev_i = None
            prev_a1 = None
            t_a1_war = {}
            for jc in range(DC):
                b = jc % 2
                tl = syn.dma([t_wdq.get(jc - 2), t_wdi.get(jc - 2),
                              t_a1_war.get(jc - 2)],
                             lambda e, jc=jc, b=b: e.dma_start(
                    out=wd_stage[b][:, :], in_=wd_s[jc * 128:(jc + 1) * 128, :]))
                t_red = dve.op([tl], lambda e, b=b: e.tensor_reduce(
                    scl[:, 2:3], wd_stage[b][:, :], AXX, ALU.max,
                    apply_absolute_value=True))
                t_swd[jc] = dve.op([t_red], lambda e, jc=jc: e.tensor_scalar(
                    scl[:, 14 + jc:15 + jc], scl[:, 2:3], R127, None, ALU.mult))
                rws = 8
                t_rw = dve.op([t_swd[jc]],
                              lambda e, jc=jc, rws=rws: e.reciprocal(
                                  scl[:, rws:rws + 1], scl[:, 14 + jc:15 + jc]))
                kb2 = jc % 2
                t_i = dve.op([t_rw, prev_i],
                             lambda e, b=b: e.tensor_scalar(
                                 i32b2[:, :],
                                 wd_stage[b][:, 0:1408],
                                 scl[:, rws:rws + 1], None, ALU.mult))
                t_k0 = dve.op([t_i, t_wdtp.get(jc - 2)],
                              lambda e, kb2=kb2: e.tensor_copy(
                                  ktm_wd[kb2][:, 0:1408], i32b2[:, :]))
                prev_i = t_k0
                t_wdi[jc] = t_i
                t_i1 = dve.op([t_k0],
                              lambda e, b=b, rws=rws: e.tensor_scalar(
                                  i32b2[:, :],
                                  wd_stage[b][:, 1408:2816],
                                  scl[:, rws:rws + 1], None, ALU.mult))
                t_k1 = dve.op([t_i1, t_wdtp.get(jc - 2)],
                              lambda e, kb2=kb2: e.tensor_copy(
                                  ktm_wd[kb2][:, 1408:2816], i32b2[:, :]))
                prev_i = t_k1
                t_a1_war[jc] = t_k1
                tk_half = [t_k0, t_k1]
                t_wdq[jc] = tk_half[-1]
                tpq = act if jc % 2 == 0 else syn
                t_wdtp[jc] = tpq.dma(tk_half, lambda e, jc=jc, kb2=kb2: e.dma_start_transpose(
                    kwdT[jc].rr("p (a b) -> p a b", a=CI), ktm_wd[kb2][:, :]))

            t_grow2 = dve.op([t_ch2], lambda e: e.tensor_single_scalar(
                gm_row[:, :], ones_f32r[:, :], g11[0:1, 0:1], ALU.mult))
            t_b3h = pool.dma([t_grow2], lambda e: e.dma_start(
                out=bh_dram.rearrange("a b -> b a"), in_=gm_row[:, :]))
            t_b4h = pool.dma([t_b3h], lambda e: e.dma_start(
                out=scl[:, 6:7], in_=bh_dram))
            t_t1h = dve.op([t_b4h, t_sig], lambda e: e.tensor_mul(
                scl[:, 7:8], scl[:, 6:7], sig_sb[:, 1:2]))
            t_shc = dve.op([t_t1h], lambda e: e.tensor_scalar(
                scl[:, 31:32], scl[:, 7:8], R127, None, ALU.mult))
            t_rhc = dve.op([t_shc], lambda e: e.reciprocal(scl[:, 30:31], scl[:, 31:32]))
            t_s2 = {}
            for jc in range(DC):
                t_s2[jc] = dve.op([t_swd[jc], t_shc], lambda e, jc=jc: e.tensor_mul(
                    scl[:, 22 + jc:23 + jc], scl[:, 14 + jc:15 + jc],
                    scl[:, 31:32]))
        for nm in (["wds0", "wds1", "ktmwd0", "ktmwd1", "i32b2", "wd32m_a", "wd32m_b"] +
                   [f"wo{d}" for d in range(DC)] + [f"yb{d}" for d in range(DC)]):
            ar.release(nm)

        softbar("D")
        khtb = [ar.alloc(f"khtb{i}", CI * TBW, BF16) for i in range(2)]
        h_stage = [ar.alloc(f"hstg{i}", TBW, F32) for i in range(6)]
        ud_scr = [ar.alloc(f"udscr{i}", TBW, F32) for i in range(3)]
        i32d = [ar.alloc(f"i32d{i}", TBW, I32) for i in range(3)]
        g_scr = [ar.alloc(f"gscr{i}", TBW, BF16) for i in range(4)]
        sg_d = [ar.alloc(f"sgd{i}", TBW, F32) for i in range(4)]
        o_scr = [ar.alloc(f"oscr{i}", TBW, F32) for i in range(4)]
        xt_d = [ar.alloc(f"xtd{i}", TBW, F32) for i in range(4)]

        if True:
            dn_ps = [psumA[:, :], psumB[:, :]]
            tok_b2_act = (act.sem, act.n)
            tok_kh = {}
            tok_mmD = {}
            tok_oD = {}
            tok_gD = {}
            tok_sg_prev = {}
            tok_o_prev = {}
            tok_od_prev = {}

            ci_list = [CI - 2, CI - 1] + list(range(CI - 2))

            def quant_d(tb):
                kb = tb % 2
                for idx, ci in enumerate(ci_list):
                    s3 = tb * CI + idx
                    hb = s3 % 6
                    ub = s3 % 3
                    if ci >= CI - 2:
                        t_u = act.op([t_rhc, t_zc, tok_kh.get(s3 - 3)],
                                     lambda e, ci=ci, tb=tb, ub=ub: e.activation(
                                         ud_scr[ub][:, :],
                                         h_tile[ci % 2][:, tb * TBW:(tb + 1) * TBW],
                                         AF.Copy, bias=0.0, scale=scl[:, 30:31]))
                    else:
                        t_hl = syn.dma([tok_kh.get(s3 - 6)],
                                       lambda e, tb=tb, ci=ci, hb=hb: e.dma_start(
                                           out=h_stage[hb][:, :],
                                           in_=h_dram[ci * 128:(ci + 1) * 128,
                                                      tb * TBW:(tb + 1) * TBW]))
                        t_u = act.op([t_hl, t_rhc, t_zc, tok_kh.get(s3 - 3)],
                                     lambda e, hb=hb, ub=ub: e.activation(
                                         ud_scr[ub][:, :], h_stage[hb][:, :], AF.Copy,
                                         bias=0.0, scale=scl[:, 30:31]))
                    qeng = dve
                    t_i = qeng.op([t_u, tok_kh.get(s3 - 3)],
                                  lambda e, ub=ub: e.tensor_scalar(
                                      i32d[ub][:, :], ud_scr[ub][:, :],
                                      CLIP_LO, CLIP_HI, ALU.max, ALU.min))
                    tok_kh[s3] = qeng.op([t_i, tok_mmD.get((tb - 2, 3))],
                                         lambda e, kb=kb, ci=ci, ub=ub: e.tensor_copy(
                                             khtb[kb][:, ci * TBW:(ci + 1) * TBW],
                                             i32d[ub][:, :]))

            quant_d(0)
            for tb in range(NTB):
                kb = tb % 2
                if tb + 1 < NTB:
                    quant_d(tb + 1)

                for jq in range(4):
                    jcs = [2 * jq, 2 * jq + 1]
                    psrc = dn_ps[jq // 2]
                    pcol = (jq % 2) * 2 * TBW
                    pe.waits([tok_gD.get((tb - 1, jc)) for jc in jcs])
                    pe.waits([t_wdtp[jc] for jc in jcs] + [tok_b2_act])
                    tok_mm = None
                    for idx, ci in enumerate(ci_list):
                        pe.waits([tok_kh[tb * CI + idx]])
                        for jc in jcs:
                            lastm = (idx == CI - 1 and jc == jcs[-1])
                            tok_mm = pe.op([], lambda e, psrc=psrc, pcol=pcol, jc=jc,
                                           ci=ci, kb=kb, jq=jq, idx=idx: e.matmul(
                                psrc[:, pcol + (jc - 2 * jq) * TBW:
                                     pcol + (jc - 2 * jq + 1) * TBW],
                                kwdT[jc][:, ci * 128:(ci + 1) * 128],
                                khtb[kb][:, ci * TBW:(ci + 1) * TBW],
                                start=(idx == 0), stop=(idx == CI - 1)),
                                inc=1 if lastm else 0)
                    tok_mmD[(tb, jq)] = tok_mm
                    for jc in jcs:
                        eb = jc % 4
                        t_xl = pool.dma([tok_o_prev.get(eb)],
                                        lambda e, tb=tb, jc=jc, eb=eb: e.dma_start(
                                            out=xt_d[eb][:, :],
                                            in_=xT[jc * 128:(jc + 1) * 128,
                                                   tb * TBW:(tb + 1) * TBW]))
                        t_g = act.op([tok_mm, tok_sg_prev.get(eb), t_zc,
                                      t_s2[jc]],
                                     lambda e, psrc=psrc, pcol=pcol, jq=jq, jc=jc,
                                     eb=eb: e.activation(
                                         g_scr[eb][:, :],
                                         psrc[:, pcol + (jc - 2 * jq) * TBW:
                                              pcol + (jc - 2 * jq + 1) * TBW],
                                         AF.Silu, bias=zero_col[:, 0:1],
                                         scale=scl[:, 22 + jc:23 + jc]))
                        tok_gD[(tb, jc)] = t_g
                        t_sg = dve.op([t_g, tok_o_prev.get(eb)],
                                      lambda e, jc=jc, tb=tb, eb=eb: e.tensor_mul(
                                          sg_d[eb][:, :], g_scr[eb][:, :],
                                          s_sb[jc][:, tb * TBW:(tb + 1) * TBW]))
                        tok_sg_prev[eb] = t_sg
                        o_eng = dve if tb == NTB - 1 else pool
                        t_o = o_eng.op([t_sg, t_xl, tok_od_prev.get(eb)],
                                       lambda e, eb=eb: e.tensor_add(
                                           o_scr[eb][:, :], sg_d[eb][:, :],
                                           xt_d[eb][:, :]))
                        tok_o_prev[eb] = t_o
                        odq = act if jc % 2 == 0 else syn
                        t_od = odq.dma([t_o], lambda e, tb=tb, jc=jc, eb=eb: e.dma_start(
                            out=outT[jc * 128:(jc + 1) * 128, tb * TBW:(tb + 1) * TBW],
                            in_=o_scr[eb][:, :]))
                        tok_oD[(tb, jc)] = t_od
                        tok_od_prev[eb] = t_od

            syn.waits([tok_oD[(tb, jc)] for tb in range(NTB) for jc in range(DC)])
            flush("ALL")

    return nc


_NC_CACHE = None


def kernel(L0_attn_x, L0_attn_Wq, L0_attn_Wv, L0_attn_Wg, L0_attn_Wo,
           L0_attn_beta, L0_attn_ln_gamma, L0_attn_ln_beta,
           L0_ffn_clip_x, L0_ffn_clip_down,
           L0_ffn_w_gate, L0_ffn_w_up, L0_ffn_w_down):
    global _NC_CACHE
    x = np.asarray(L0_attn_x, np.float32)
    wq = np.asarray(L0_attn_Wq, np.float32)
    wv = np.asarray(L0_attn_Wv, np.float32)
    wg = np.asarray(L0_attn_Wg, np.float32)
    wo = np.asarray(L0_attn_Wo, np.float32)
    beta = np.asarray(L0_attn_beta, np.float32)
    gam = np.asarray(L0_attn_ln_gamma, np.float32)
    bln = np.asarray(L0_attn_ln_beta, np.float32)
    cx = np.asarray(L0_ffn_clip_x, np.float32)
    cd = np.asarray(L0_ffn_clip_down, np.float32)
    wgate = np.ascontiguousarray(np.asarray(L0_ffn_w_gate, np.float32))
    wup = np.ascontiguousarray(np.asarray(L0_ffn_w_up, np.float32))
    wdown = np.ascontiguousarray(np.asarray(L0_ffn_w_down, np.float32))

    wqT = np.ascontiguousarray(wq.T)
    wvT = np.ascontiguousarray(wv.T)
    wgT = np.ascontiguousarray(wg.T)
    woT = np.ascontiguousarray(wo.T)
    beta_pc = np.ascontiguousarray(np.repeat(beta.reshape(DC, 2), 64, axis=1).T)
    gam_pc = np.ascontiguousarray(gam.reshape(DC, 128).T)
    bln_pc = np.ascontiguousarray(bln.reshape(DC, 128).T)
    clip2 = np.ascontiguousarray(np.repeat(
        np.concatenate([cx.reshape(1), cd.reshape(1)]).reshape(1, 2), 128, axis=0))

    if _NC_CACHE is None:
        _NC_CACHE = build_nc()
    nc = _NC_CACHE

    in_maps = []
    for b in range(NCORES):
        in_maps.append({
            "xT": np.ascontiguousarray(x[b].T),
            "wqT": wqT, "wvT": wvT, "wgT": wgT, "woT": woT,
            "wg_s": wgate, "wu_s": wup, "wd_s": wdown,
            "beta_pc": beta_pc, "gam_pc": gam_pc, "bln_pc": bln_pc,
            "clip2": clip2,
        })
    res = run_bass_kernel_spmd(nc, in_maps, core_ids=list(range(NCORES)))
    out = np.stack([np.asarray(res.results[b]["outT"]).T for b in range(NCORES)], axis=0)
    return np.ascontiguousarray(out.astype(np.float32))


# version 42
# speedup vs baseline: 1.0249x; 1.0184x over previous
import contextlib
import numpy as np

import bass_rust
import concourse.bass as bass
import concourse.mybir as mybir
from concourse.bass_utils import run_bass_kernel_spmd

F32 = mybir.dt.float32
BF16 = mybir.dt.bfloat16
I32 = mybir.dt.int32
AF = mybir.ActivationFunctionType
ALU = mybir.AluOpType
AXX = mybir.AxisListType.X
AXC = mybir.AxisListType.C
RO_ABSMAX = bass_rust.ReduceOp.absmax

T = 2048
D = 1024
DC = 8
II = 2816
CI = 22
TBW = 512
NTB = 4
NCORES = 8
LN_EPS = 1e-5
R127 = float(np.float32(1.0) / np.float32(127.0))
CLIP_LO = -128.49
CLIP_HI = 127.49
RND_MAGIC = 12582912.0
ARENA_W = 52400
PE_FULL_TAGS = frozenset()


class Eng:
    def __init__(self, name, blockname, sem, dma_sems=()):
        self.name = name
        self.blockname = blockname
        self.sem = sem
        self.n = 0
        self.seen = {}
        self.items = []
        self.dch = [[s, 0] for s in dma_sems]
        self.rr = 0

    def _prep(self, deps):
        ws = []
        for d in deps:
            if d is None:
                continue
            sem, val = d
            k = id(sem)
            if self.seen.get(k, 0) >= val:
                continue
            self.seen[k] = val
            ws.append((sem, val))
        return ws

    def op(self, deps, fn, inc=1):
        ws = self._prep(deps)
        tok = None
        if inc:
            self.n += inc
            tok = (self.sem, self.n)
        sem = self.sem

        def item(e, ws=ws, fn=fn, inc=inc, sem=sem):
            for (s, v) in ws:
                e.wait_ge(s, v)
            inst = fn(e)
            if inc:
                inst.then_inc(sem, inc)

        self.items.append(item)
        return tok

    def dma(self, deps, fn, inc=16):
        ch = self.rr % len(self.dch)
        self.rr += 1
        entry = self.dch[ch]
        sem_c, prev = entry[0], entry[1]
        deps = list(deps)
        if prev:
            deps.append((sem_c, prev))
        ws = self._prep(deps)
        entry[1] = prev + inc
        tok = (sem_c, entry[1])

        def item(e, ws=ws, fn=fn, sem_c=sem_c, inc=inc):
            for (s, v) in ws:
                e.wait_ge(s, v)
            fn(e).then_inc(sem_c, inc)

        self.items.append(item)
        return tok

    def waits(self, deps):
        ws = self._prep(deps)
        if not ws:
            return

        def item(e, ws=ws):
            for (s, v) in ws:
                e.wait_ge(s, v)

        self.items.append(item)


class TV:

    def __init__(self, h, c0, ncols, rows):
        self.h = h
        self.c0 = c0
        self.ncols = ncols
        self.rows = rows

    def __getitem__(self, key):
        ps, cs = key
        if self.rows == 1:
            ps = slice(0, 1)
        start = cs.start if cs.start is not None else 0
        stop = cs.stop if cs.stop is not None else self.ncols
        return self.h[ps, self.c0 + start:self.c0 + stop]

    def rr(self, spec, **kw):
        return self.h[:, self.c0:self.c0 + self.ncols].rearrange(spec, **kw)


class Arena:
    def __init__(self, h32):
        self.h = {F32: h32, BF16: h32.bitcast(BF16), I32: h32.bitcast(I32)}
        self.mult = {F32: 1, BF16: 2, I32: 1}
        self.free = [[0, ARENA_W]]
        self.live = {}

    def alloc(self, name, elems, dt, rows=128):
        m = self.mult[dt]
        words = -(-elems // m)
        words = -(-words // 16) * 16
        for i, (off, n) in enumerate(self.free):
            if n >= words:
                if n == words:
                    self.free.pop(i)
                else:
                    self.free[i] = [off + words, n - words]
                self.live[name] = (off, words)
                return TV(self.h[dt], off * m, elems, rows)
        raise RuntimeError(f"arena OOM for {name}: need {words}w, free={self.free}")

    def view(self, name, elems, dt, rows=128, word_off=0):
        off, words = self.live[name]
        m = self.mult[dt]
        return TV(self.h[dt], (off + word_off) * m, elems, rows)

    def release(self, name):
        off, words = self.live.pop(name)
        self.free.append([off, words])
        self.free.sort()
        merged = []
        for seg in self.free:
            if merged and merged[-1][0] + merged[-1][1] == seg[0]:
                merged[-1][1] += seg[1]
            else:
                merged.append(list(seg))
        self.free = merged


def build_nc():
    nc = bass.Bass("TRN2", target_bir_lowering=False, debug=False)

    xT = nc.dram_tensor("xT", [D, T], F32, kind="ExternalInput").ap()
    wqT = nc.dram_tensor("wqT", [D, D], F32, kind="ExternalInput").ap()
    wvT = nc.dram_tensor("wvT", [D, D], F32, kind="ExternalInput").ap()
    wgT = nc.dram_tensor("wgT", [D, D], F32, kind="ExternalInput").ap()
    woT = nc.dram_tensor("woT", [D, D], F32, kind="ExternalInput").ap()
    wg_s = nc.dram_tensor("wg_s", [II, D], F32, kind="ExternalInput").ap()
    wu_s = nc.dram_tensor("wu_s", [II, D], F32, kind="ExternalInput").ap()
    wd_s = nc.dram_tensor("wd_s", [D, II], F32, kind="ExternalInput").ap()
    beta_pc = nc.dram_tensor("beta_pc", [128, DC], F32, kind="ExternalInput").ap()
    gam_pc = nc.dram_tensor("gam_pc", [128, DC], F32, kind="ExternalInput").ap()
    bln_pc = nc.dram_tensor("bln_pc", [128, DC], F32, kind="ExternalInput").ap()
    clip2 = nc.dram_tensor("clip2", [128, 2], F32, kind="ExternalInput").ap()

    outT = nc.dram_tensor("outT", [D, T], F32, kind="ExternalOutput").ap()

    h_dram = nc.dram_tensor("h_dram", [II, T], F32).ap()
    xn_dram = nc.dram_tensor("xn_dram", [D, T], F32).ap()
    rows_bounce = nc.dram_tensor("rows_bounce", [4, T], F32).ap()
    bx_dram = nc.dram_tensor("bx_dram", [128, 1], F32).ap()
    bh_dram = nc.dram_tensor("bh_dram", [128, 1], F32).ap()
    ccx_in = nc.dram_tensor("ccx_in", [1, 1], F32).ap()
    ccx_out = nc.dram_tensor("ccx_out", [1, 1], F32, addr_space="Shared").ap()
    cch_in = nc.dram_tensor("cch_in", [1, 1], F32).ap()
    cch_out = nc.dram_tensor("cch_out", [1, 1], F32, addr_space="Shared").ap()

    es = contextlib.ExitStack()
    with es:
        sems = {n: es.enter_context(nc.semaphore("sem_" + n))
                for n in ["pe", "act", "dve", "pool", "syn"]}
        dsem = {q: [es.enter_context(nc.semaphore(f"dma_{q}{i}")) for i in range(k)]
                for q, k in [("syn", 6), ("act", 4), ("pool", 3)]}
        pe = Eng("pe", "tensor", sems["pe"])
        act = Eng("act", "scalar", sems["act"], dsem["act"])
        dve = Eng("dve", "vector", sems["dve"])
        pool = Eng("pool", "gpsimd", sems["pool"], dsem["pool"])
        syn = Eng("syn", "sync", sems["syn"], dsem["syn"])
        engines = [pe, act, dve, pool, syn]

        def flush(tag):
            used = [e for e in engines if e.items]
            if not used:
                return
            with nc.Block() as blk:
                for e in used:
                    items = e.items
                    e.items = []
                    getattr(blk, e.blockname)(
                        lambda eng, items=items: [it(eng) for it in items])

        def softbar(tag=None):
            toks = [(e.sem, e.n) for e in engines]
            for e in engines:
                toks += [(c[0], c[1]) for c in e.dch if c[1]]
            for e in engines:
                if e is pe and tag not in PE_FULL_TAGS:
                    continue
                e.waits([t for t in toks if t[0] is not e.sem])

        def SB(name, shape, dt):
            return es.enter_context(nc.sbuf_tensor(name, shape, dt))

        ones_f32r = SB("ones_f32r", [1, 128], F32)
        ones_bf16c = SB("ones_bf16c", [128, 1], BF16)
        ones_bf16r = SB("ones_bf16r", [1, 128], BF16)
        zero_col = SB("zero_col", [128, 1], F32)
        eps_col = SB("eps_col", [128, 1], F32)
        clip_sb = SB("clip_sb", [128, 2], F32)
        sig_sb = SB("sig_sb", [128, 2], F32)
        betab = SB("betab", [128, DC], F32)
        lamcols = SB("lamcols", [128, DC], F32)
        gcols = SB("gcols", [128, DC], F32)
        bcols = SB("bcols", [128, DC], F32)
        scl = SB("scl", [128, 64], F32)
        g11 = SB("g11", [1, 4], F32)
        gm_row = SB("gm_row", [1, 128], F32)

        arena_h = es.enter_context(nc.sbuf_tensor("arena", [128, ARENA_W], F32))
        ar = Arena(arena_h)
        psumA = es.enter_context(nc.psum_tensor("psumA", [128, T], F32))
        psumB = es.enter_context(nc.psum_tensor("psumB", [128, T], F32))

        ones_big = ar.alloc("ones_big", T, F32)
        wstage = [ar.alloc(f"wstg{i}", D, F32) for i in range(2)]

        t_c2 = syn.dma([], lambda e: e.dma_start(out=clip_sb[:, :], in_=clip2))
        t_bp = syn.dma([], lambda e: e.dma_start(out=betab[:, :], in_=beta_pc))
        t_gp = syn.dma([], lambda e: e.dma_start(out=gcols[:, :], in_=gam_pc))
        t_bl = syn.dma([], lambda e: e.dma_start(out=bcols[:, :], in_=bln_pc))
        t_ones = dve.op([], lambda e: e.memset(ones_big[:, :], 1.0))
        t_o32 = dve.op([], lambda e: e.memset(ones_f32r[:, :], 1.0))
        t_obc = dve.op([], lambda e: e.memset(ones_bf16c[:, :], 1.0))
        t_obr = dve.op([], lambda e: e.memset(ones_bf16r[:, :], 1.0))
        t_zc = dve.op([], lambda e: e.memset(zero_col[:, :], 0.0))
        t_ec = dve.op([], lambda e: e.memset(eps_col[:, :], LN_EPS))
        t_sig = act.op([t_c2], lambda e: e.activation(sig_sb[:, :], clip_sb[:, :], AF.Sigmoid))
        t_lam = act.op([t_bp], lambda e: e.activation(lamcols[:, :], betab[:, :], AF.Sigmoid))

        xnb = [ar.alloc(f"xnb{d}", T, BF16) for d in range(DC)]
        wv_sb = [ar.alloc(f"wv{d}", D, BF16) for d in range(DC)]
        wq_sb = [ar.alloc(f"wq{d}", D, BF16) for d in range(DC)]

        t_wload = {}
        t_wstage_rd = {}
        wsi = 0

        def emit_wpair(d):
            nonlocal wsi
            for wname, src, dst, ceng in [("wv", wvT, wv_sb, act),
                                          ("wq", wqT, wq_sb, dve)]:
                b = wsi % 2
                tl = pool.dma([t_wstage_rd.get(wsi - 2)],
                              lambda e, s=src, d=d, b=b: e.dma_start(
                                  out=wstage[b][:, :], in_=s[d * 128:(d + 1) * 128, :]))
                if ceng is act:
                    tc = act.op([tl], lambda e, d=d, b=b, dst=dst: e.copy(
                        dst[d][:, :], wstage[b][:, :]))
                else:
                    tc = dve.op([tl], lambda e, d=d, b=b, dst=dst: e.tensor_copy(
                        dst[d][:, :], wstage[b][:, :]))
                t_wload[(wname, d)] = tc
                t_wstage_rd[wsi] = tc
                wsi += 1

        x_stage = [ar.alloc(f"xstg{i}", T, F32) for i in range(3)]
        M_b = ar.alloc("M_b", T, F32)
        R_b = ar.alloc("R_b", T, F32)
        scrX = [ar.alloc(f"scrX{i}", T, F32) for i in range(2)]
        xnX = [ar.alloc(f"xnX{i}", T, F32) for i in range(2)]
        xb2 = [ar.alloc(f"xb2_{i}", T, BF16) for i in range(2)]
        xq2 = [ar.alloc(f"xq2_{i}", T, BF16) for i in range(2)]
        rows = [ar.alloc(f"rowA{i}", T, F32, rows=1) for i in range(4)]
        rA, rB, rC, rD = rows

        t_x1 = {}
        t_mmA = {}
        if True:
            ps_sx = psumA[0:1, :]
            ps_sx2 = psumB[0:1, :]
            tok_stats = None
            for d in range(DC):
                b = d % 2
                b3 = d % 3
                t_x1[d] = syn.dma([t_mmA.get(d - 3)], lambda e, d=d, b3=b3: e.dma_start(
                    out=x_stage[b3][:, :], in_=xT[d * 128:(d + 1) * 128, :]))
                tcb = act.op([t_x1[d], t_mmA.get(d - 2)], lambda e, b=b, b3=b3: e.copy(
                    xb2[b][:, :], x_stage[b3][:, :]))
                tcs = dve.op([t_x1[d], t_mmA.get(d - 2)],
                             lambda e, b=b, b3=b3: e.tensor_mul(
                                 xq2[b][:, :], x_stage[b3][:, :], x_stage[b3][:, :]))
                emit_wpair(d)
                pe.waits([tcb, tcs, t_obc])
                for tb in range(NTB):
                    pe.op([], lambda e, b=b, tb=tb, d=d: e.matmul(
                        ps_sx[0:1, tb * TBW:(tb + 1) * TBW], ones_bf16c[:, :],
                        xb2[b][:, tb * TBW:(tb + 1) * TBW],
                        start=(d == 0), stop=(d == DC - 1)), inc=0)
                    last = (tb == NTB - 1)
                    tmm = pe.op([], lambda e, b=b, tb=tb, d=d: e.matmul(
                        ps_sx2[0:1, tb * TBW:(tb + 1) * TBW], ones_bf16c[:, :],
                        xq2[b][:, tb * TBW:(tb + 1) * TBW],
                        start=(d == 0), stop=(d == DC - 1)),
                        inc=1 if last else 0)
                    if last:
                        t_mmA[d] = tmm
                        tok_stats = tmm
            t_ev1 = act.op([tok_stats], lambda e: e.copy(rA[:, :], ps_sx[:, :]))
            t_ev2 = act.op([tok_stats], lambda e: e.copy(rB[:, :], ps_sx2[:, :]))
            t_mu = dve.op([t_ev1], lambda e: e.tensor_scalar(
                rC[:, :], rA[:, :], 1.0 / D, None, ALU.mult))
            t_e2 = dve.op([t_ev2, t_mu], lambda e: e.tensor_scalar(
                rA[:, :], rB[:, :], 1.0 / D, None, ALU.mult))
            t_m2 = act.op([t_mu, t_ev2, t_e2], lambda e: e.activation(
                rB[:, :], rC[:, :], AF.Square))
            t_var = dve.op([t_e2, t_m2], lambda e: e.tensor_sub(
                rD[:, :], rA[:, :], rB[:, :]))
            t_sd = act.op([t_var, t_ec, t_e2], lambda e: e.activation(
                rA[:, :], rD[:, :], AF.Sqrt, bias=eps_col[0:1, 0:1], scale=1.0))
            t_rstd = dve.op([t_sd, t_var], lambda e: e.reciprocal(rD[:, :], rA[:, :]))

        if True:
            def bc128(row_ap):
                return bass.AP(row_ap.tensor, row_ap.offset,
                               [[0, 128]] + [list(dd) for dd in row_ap.ap[1:]])

            t_sp1 = act.dma([t_mu], lambda e: e.dma_start(
                out=rows_bounce[0:1, :], in_=rC[:, :]))
            t_sp2 = syn.dma([t_rstd], lambda e: e.dma_start(
                out=rows_bounce[1:2, :], in_=rD[:, :]))
            t_Mb = act.dma([t_sp1], lambda e: e.dma_start(
                out=M_b[:, :], in_=bc128(rows_bounce[0:1, :])))
            t_Rb = syn.dma([t_sp2], lambda e: e.dma_start(
                out=R_b[:, :], in_=bc128(rows_bounce[1:2, :])))

            gslot = 0
            t_gm = None
            prev = {}
            for d in range(DC):
                b = d % 2
                b3 = d % 3
                t_x2 = syn.dma([prev.get(("sub", d - 3)), tok_stats],
                               lambda e, d=d, b3=b3: e.dma_start(
                                   out=x_stage[b3][:, :],
                                   in_=xT[d * 128:(d + 1) * 128, :]))
                s2eng = pool if d in (2, 5) else dve
                t_sub = s2eng.op([t_x2, t_Mb, prev.get(("mul", d - 2))],
                                 lambda e, b=b, b3=b3: e.tensor_sub(
                                     scrX[b][:, :], x_stage[b3][:, :], M_b[:, :]))
                prev[("sub", d)] = t_sub
                t_xn = s2eng.op([t_sub, t_Rb, prev.get(("cast", d - 2)),
                                 prev.get(("rm", d - 2)), prev.get(("spill", d - 2))],
                                lambda e, b=b: e.tensor_mul(
                                    xnX[b][:, :], scrX[b][:, :], R_b[:, :]))
                prev[("mul", d)] = t_xn
                prev[("cast", d)] = act.op([t_xn], lambda e, d=d, b=b: e.copy(
                    xnb[d][:, :], xnX[b][:, :]))
                rm = dve.op([t_xn], lambda e, d=d, b=b: e.tensor_reduce(
                    scl[:, 2 + b:3 + b], xnX[b][:, :], AXX, ALU.max,
                    apply_absolute_value=True))
                prev[("rm", d)] = rm
                prev[("spill", d)] = act.dma([t_xn], lambda e, d=d, b=b: e.dma_start(
                    out=xn_dram[d * 128:(d + 1) * 128, :], in_=xnX[b][:, :]))
                if d == 0:
                    t_gm = dve.op([rm], lambda e: e.tensor_copy(scl[:, 0:1], scl[:, 2:3]))
                else:
                    ns = 1 - gslot
                    t_gm = dve.op([rm, t_gm],
                                  lambda e, b=b, g=gslot, ns=ns: e.tensor_tensor(
                                      scl[:, ns:ns + 1], scl[:, g:g + 1],
                                      scl[:, 2 + b:3 + b], ALU.max))
                    gslot = ns
            t_xnb_last = prev[("cast", DC - 1)]
        for nm in (["xstg0", "xstg1", "xstg2", "M_b", "R_b", "scrX0", "scrX1", "xnX0",
                    "xnX1", "xb2_0", "xb2_1", "xq2_0", "xq2_1",
                    "rowA0", "rowA1", "rowA2", "rowA3"]):
            ar.release(nm)

        softbar("C1")
        i32scr = ar.alloc("i32scr", T, I32)
        kx = [ar.alloc(f"kx{d}", T, BF16) for d in range(DC)]
        qscr = ar.alloc("qscr", T, F32)
        xn_stage = [ar.alloc(f"xnstg{i}", T, F32) for i in range(2)]
        lamb = ar.alloc("lamb", T, F32)
        v_sb = ar.alloc("v_sb", T, F32)
        states = ar.alloc("states", T, F32)
        yb = [ar.alloc(f"yb{d}", T, BF16) for d in range(DC)]

        t_b1 = pool.dma([t_gm], lambda e, g=gslot: e.dma_start(
            out=bx_dram, in_=scl[:, g:g + 1]))
        t_b2 = pool.dma([t_b1], lambda e: e.dma_start(
            out=gm_row[:, :], in_=bx_dram.rearrange("a b -> b a")))
        t_gx = dve.op([t_b2], lambda e: e.tensor_reduce(
            g11[:, 0:1], gm_row[:, :], AXX, ALU.max))
        t_cx1 = pool.dma([t_gx], lambda e: e.dma_start(out=ccx_in, in_=g11[:, 0:1]))
        t_cc1 = pool.dma([t_cx1], lambda e: e.collective_compute(
            "AllReduce", ALU.max, replica_groups=[list(range(NCORES))],
            ins=[ccx_in.opt()], outs=[ccx_out.opt()]), inc=1)
        t_cx2 = pool.dma([t_cc1], lambda e: e.dma_start(
            out=g11[:, 0:1], in_=ccx_out))

        t_grow = pool.op([t_cx2, t_o32], lambda e: e.tensor_single_scalar(
            gm_row[:, :], ones_f32r[:, :], g11[0:1, 0:1], ALU.mult))
        t_b3 = pool.dma([t_grow], lambda e: e.dma_start(
            out=bx_dram.rearrange("a b -> b a"), in_=gm_row[:, :]))
        t_b4 = pool.dma([t_b3], lambda e: e.dma_start(
            out=scl[:, 6:7], in_=bx_dram))
        t_t1c = pool.op([t_b4, t_sig], lambda e: e.tensor_mul(
            scl[:, 7:8], scl[:, 6:7], sig_sb[:, 0:1]))
        t_sxc = pool.op([t_t1c], lambda e: e.tensor_scalar(
            scl[:, 5:6], scl[:, 7:8], R127, None, ALU.mult))

        if True:
            v_ps = psumA[:, :]
            q_ps = psumB[:, :]
            t_scan = None
            t_vev = None
            t_yTT = None
            t_yTTs = {}
            t_rxc = None
            prevq = {}
            for c in range(DC):
                t_lamb = act.op([t_lam, t_ones, t_zc, t_scan],
                                lambda e, c=c: e.activation(
                                    lamb[:, :], ones_big[:, :], AF.Copy,
                                    bias=0.0, scale=lamcols[:, c:c + 1]))
                pe.waits([t_vev, t_Rb, t_xnb_last])
                tok_v = None
                for d in range(DC):
                    for tb in range(NTB):
                        lastv = (d == DC - 1 and tb == NTB - 1)
                        tok_v = pe.op([], lambda e, c=c, d=d, tb=tb: e.matmul(
                            v_ps[:, tb * TBW:(tb + 1) * TBW],
                            wv_sb[d][:, c * 128:(c + 1) * 128],
                            xnb[d][:, tb * TBW:(tb + 1) * TBW],
                            start=(d == 0), stop=(d == DC - 1)),
                            inc=1 if lastv else 0)
                t_vev = act.op([tok_v, t_scan], lambda e: e.copy(v_sb[:, :], v_ps[:, :]))
                t_scan = dve.op([t_vev, t_lamb, t_yTT],
                                lambda e: e.tensor_tensor_scan(
                                    states[:, :], lamb[:, :], v_sb[:, :], 0.0,
                                    ALU.mult, ALU.add))
                pe.waits([t_yTT])
                tok_q = None
                for d in range(DC):
                    for tb in range(NTB):
                        lastq = (d == DC - 1 and tb == NTB - 1)
                        tok_q = pe.op([], lambda e, c=c, d=d, tb=tb: e.matmul(
                            q_ps[:, tb * TBW:(tb + 1) * TBW],
                            wq_sb[d][:, c * 128:(c + 1) * 128],
                            xnb[d][:, tb * TBW:(tb + 1) * TBW],
                            start=(d == 0), stop=(d == DC - 1)),
                            inc=1 if lastq else 0)
                t_yTT = dve.op([tok_q, t_scan], lambda e, c=c: e.tensor_mul(
                    yb[c][:, :], q_ps[:, :], states[:, :]))
                t_yTTs[c] = t_yTT

                kx_sched = {3: [0], 4: [1, 2], 5: [3, 4], 6: [5, 6], 7: [7]}
                if c >= 3:
                    if c == 3:
                        t_rxc = dve.op([t_sxc], lambda e: e.reciprocal(
                            scl[:, 4:5], scl[:, 5:6]))
                    for d in kx_sched[c]:
                        bq = d % 2
                        t_xl = syn.dma([prevq.get(("ts1", d - 2))],
                                       lambda e, d=d, bq=bq: e.dma_start(
                                           out=xn_stage[bq][:, :],
                                           in_=xn_dram[d * 128:(d + 1) * 128, :]))
                        t_q1 = dve.op([t_xl, t_rxc, prevq.get(("ts2", d - 1))],
                                      lambda e, bq=bq: e.tensor_scalar(
                                          qscr[:, :], xn_stage[bq][:, :],
                                          scl[:, 4:5], CLIP_HI, ALU.mult, ALU.min))
                        prevq[("ts1", d)] = t_q1
                        t_q2 = dve.op([t_q1, prevq.get(("cp", d - 1))],
                                      lambda e: e.tensor_scalar(
                                          i32scr[:, :], qscr[:, :], CLIP_LO,
                                          None, ALU.max))
                        prevq[("ts2", d)] = t_q2
                        prevq[("cp", d)] = dve.op([t_q2], lambda e, d=d: e.tensor_copy(
                            kx[d][:, :], i32scr[:, :]))
            wg_sb = [ar.view(f"wv{d}", D, BF16) for d in range(DC)]
            for d in range(DC):
                b = wsi % 2
                tl = syn.dma([t_wstage_rd.get(wsi - 2), tok_v],
                             lambda e, d=d, b=b: e.dma_start(
                                 out=wstage[b][:, :],
                                 in_=wgT[d * 128:(d + 1) * 128, :]))
                tc = act.op([tl], lambda e, d=d, b=b: e.copy(
                    wg_sb[d][:, :], wstage[b][:, :]))
                t_wload[("wg", d)] = tc
                t_wstage_rd[wsi] = tc
                wsi += 1
        for nm in (["lamb", "v_sb", "states", "qscr", "xnstg0", "xnstg1"] +
                   [f"wq{d}" for d in range(DC)]):
            ar.release(nm)

        softbar("C2")
        NM2 = ar.alloc("NM2", T, F32)
        R2 = ar.alloc("R2", T, F32)
        ysq = [ar.alloc(f"ysq{i}", T, BF16) for i in range(2)]
        rows = [ar.alloc(f"rw{i}", T, F32, rows=1) for i in range(4)]
        rA, rB, rC, rD = rows

        if True:
            ps_sy = psumA[0:1, :]
            ps_sy2 = psumA[32:33, :]
            t_ysq_war = {}
            tok_sy = None
            for d in range(DC):
                yq = d % 2
                t_ysq = act.op([t_yTTs[d], t_ysq_war.get(d - 2)],
                               lambda e, d=d, yq=yq: e.activation(
                                   ysq[yq][:, :], yb[d][:, :], AF.Square))
                pe.waits([t_ysq, t_obc, t_vev])
                for tb in range(NTB):
                    pe.op([], lambda e, d=d, tb=tb: e.matmul(
                        ps_sy[0:1, tb * TBW:(tb + 1) * TBW], ones_bf16c[:, :],
                        yb[d][:, tb * TBW:(tb + 1) * TBW],
                        start=(d == 0), stop=(d == DC - 1)), inc=0)
                    last = (tb == NTB - 1)
                    tok_sy = pe.op([], lambda e, d=d, tb=tb, yq=yq: e.matmul(
                        ps_sy2[0:1, tb * TBW:(tb + 1) * TBW], ones_bf16c[:, :],
                        ysq[yq][:, tb * TBW:(tb + 1) * TBW],
                        start=(d == 0), stop=(d == DC - 1)),
                        inc=1 if last else 0)
                t_ysq_war[d] = tok_sy

            t_ev1 = act.op([tok_sy], lambda e: e.copy(rA[:, :], ps_sy[:, :]))
            t_ev2 = act.op([tok_sy], lambda e: e.copy(rB[:, :], ps_sy2[:, :]))
            t_mu = dve.op([t_ev1], lambda e: e.tensor_scalar(
                rC[:, :], rA[:, :], 1.0 / D, None, ALU.mult))
            t_e2 = dve.op([t_ev2, t_mu], lambda e: e.tensor_scalar(
                rA[:, :], rB[:, :], 1.0 / D, None, ALU.mult))
            t_m2 = act.op([t_mu, t_ev2, t_e2], lambda e: e.activation(
                rB[:, :], rC[:, :], AF.Square))
            t_var = dve.op([t_e2, t_m2], lambda e: e.tensor_sub(
                rD[:, :], rA[:, :], rB[:, :]))
            t_sd = act.op([t_var, t_e2], lambda e: e.activation(
                rA[:, :], rD[:, :], AF.Sqrt, bias=eps_col[0:1, 0:1], scale=1.0))
            t_rstd = dve.op([t_sd, t_var], lambda e: e.reciprocal(rD[:, :], rA[:, :]))
            t_n1 = dve.op([t_rstd, t_sd], lambda e: e.tensor_mul(
                rA[:, :], rC[:, :], rD[:, :]))
            t_n2 = dve.op([t_n1, t_m2], lambda e: e.tensor_scalar(
                rB[:, :], rA[:, :], -1.0, None, ALU.mult))

        if True:
            def bc128(row_ap):
                return bass.AP(row_ap.tensor, row_ap.offset,
                               [[0, 128]] + [list(dd) for dd in row_ap.ap[1:]])

            t_sp3 = act.dma([t_rstd], lambda e: e.dma_start(
                out=rows_bounce[2:3, :], in_=rD[:, :]))
            t_sp4 = syn.dma([t_n2], lambda e: e.dma_start(
                out=rows_bounce[3:4, :], in_=rB[:, :]))
            t_R2 = act.dma([t_sp3], lambda e: e.dma_start(
                out=R2[:, :], in_=bc128(rows_bounce[2:3, :])))
            t_NM2 = syn.dma([t_sp4], lambda e: e.dma_start(
                out=NM2[:, :], in_=bc128(rows_bounce[3:4, :])))
        for nm in ["ysq0", "ysq1", "rw0", "rw1", "rw2", "rw3"]:
            ar.release(nm)

        softbar("C3")
        ga_sb = [ar.alloc(f"ga{i}", T, BF16) for i in range(2)]
        tscr1p = [ar.alloc(f"tscr1{i}", T, BF16) for i in range(2)]
        tscr2p = [ar.alloc(f"tscr2{i}", T, BF16) for i in range(2)]
        if True:
            t_sil = {}
            t_yg = {}
            tt3_war = {}
            for c in range(DC):
                b = c % 2
                ga_ps = psumA[:, :] if b == 0 else psumB[:, :]
                pe.waits([t_sil.get(c - 2), t_ev1, t_ev2, t_yTT])
                tok_g = None
                for d in range(DC):
                    for tb in range(NTB):
                        lastg = (d == DC - 1 and tb == NTB - 1)
                        tok_g = pe.op([t_wload[("wg", d)]],
                                      lambda e, c=c, d=d, tb=tb, ga_ps=ga_ps: e.matmul(
                                          ga_ps[:, tb * TBW:(tb + 1) * TBW],
                                          wg_sb[d][:, c * 128:(c + 1) * 128],
                                          xnb[d][:, tb * TBW:(tb + 1) * TBW],
                                          start=(d == 0), stop=(d == DC - 1)),
                                      inc=1 if lastg else 0)
                t_sil[c] = act.op([tok_g, t_yg.get(c - 2)],
                                  lambda e, b=b, ga_ps=ga_ps: e.activation(
                                      ga_sb[b][:, :], ga_ps[:, :], AF.Silu))
                tt1 = dve.op([t_R2, t_yg.get(c - 2)], lambda e, c=c, b=b: e.tensor_mul(
                    tscr1p[b][:, :], yb[c][:, :], R2[:, :]))
                tt2 = dve.op([tt1, t_NM2, tt3_war.get(c - 2)],
                             lambda e, b=b: e.tensor_add(
                                 tscr2p[b][:, :], tscr1p[b][:, :], NM2[:, :]))
                tt3 = dve.op([tt2, t_gp, t_bl, t_yg.get(c - 2)],
                             lambda e, c=c, b=b: e.tensor_scalar(
                                 tscr1p[b][:, :], tscr2p[b][:, :], gcols[:, c:c + 1],
                                 bcols[:, c:c + 1], ALU.mult, ALU.add))
                tt3_war[c] = tt3
                t_yg[c] = dve.op([tt3, t_sil[c]], lambda e, c=c, b=b: e.tensor_mul(
                    yb[c][:, :], tscr1p[b][:, :], ga_sb[b][:, :]))
        for nm in (["ga0", "ga1", "tscr10", "tscr11", "tscr20", "tscr21", "NM2", "R2"] +
                   [f"wv{d}" for d in range(DC)] + [f"xnb{d}" for d in range(DC)]):
            ar.release(nm)

        softbar("B")
        wo_sb = [ar.alloc(f"wo{d}", D, BF16) for d in range(DC)]
        NPB = 3
        ws_g = [ar.alloc(f"wsg{i}", D, F32) for i in range(NPB)]
        ws_u = [ar.alloc(f"wsu{i}", D, F32) for i in range(NPB)]
        ktm_g = [ar.alloc(f"ktmg{i}", D, BF16) for i in range(NPB)]
        ktm_u = [ar.alloc(f"ktmu{i}", D, BF16) for i in range(NPB)]
        kwT_g = [ar.alloc(f"kwTg{i}", DC * 128, BF16) for i in range(NPB)]
        kwT_u = [ar.alloc(f"kwTu{i}", DC * 128, BF16) for i in range(NPB)]
        h_tile = [ar.alloc(f"htile{i}", T, F32) for i in range(2)]
        sg_scr = [ar.alloc(f"sgscr{i}", TBW, F32) for i in range(2)]
        up_scr = [ar.alloc(f"upscr{i}", TBW, F32) for i in range(2)]

        if True:
            ps_g = psumA[:, :]
            ps_u = psumB[:, :]
            tok_c3_sil = (act.sem, act.n)
            for d in range(DC):
                b = wsi % 2
                tl = syn.dma([t_wstage_rd.get(wsi - 2)],
                             lambda e, d=d, b=b: e.dma_start(
                                 out=wstage[b][:, :], in_=woT[d * 128:(d + 1) * 128, :]))
                tc = act.op([tl], lambda e, d=d, b=b: e.copy(
                    wo_sb[d][:, :], wstage[b][:, :]))
                t_wload[("wo", d)] = tc
                t_wstage_rd[wsi] = tc
                wsi += 1

            tok_wread = {}
            tok_k = {}
            tok_scg = {}
            tok_tp = {}
            tok_epi = {}
            tok_hdma = [None, None]
            tok_hread = [None, None]
            state = {"i32rd": None}
            t_hm = None
            hslot = 0

            def prep_b(ci):
                b = ci % NPB
                tlg = syn.dma([tok_wread.get(("g", ci - NPB))],
                              lambda e, ci=ci, b=b: e.dma_start(
                                  out=ws_g[b][:, :], in_=wg_s[ci * 128:(ci + 1) * 128, :]))
                tlu = syn.dma([tok_wread.get(("u", ci - NPB))],
                              lambda e, ci=ci, b=b: e.dma_start(
                                  out=ws_u[b][:, :], in_=wu_s[ci * 128:(ci + 1) * 128, :]))
                for w, tl, ws_, ktm_, kwT_, sm0, cg0, rw0 in [
                        ("g", tlg, ws_g, ktm_g, kwT_g, 32, 38, 44),
                        ("u", tlu, ws_u, ktm_u, kwT_u, 35, 41, 47)]:
                    sm = sm0 + b
                    cg = cg0 + b
                    rw = rw0 + b
                    t_red = dve.op([tl], lambda e, b=b, ws_=ws_, sm=sm: e.tensor_reduce(
                        scl[:, sm:sm + 1], ws_[b][:, :], AXX, ALU.max,
                        apply_absolute_value=True))
                    t_sw = dve.op([t_red], lambda e, sm=sm: e.tensor_scalar(
                        scl[:, sm:sm + 1], scl[:, sm:sm + 1], R127, None, ALU.mult))
                    t_rw = dve.op([t_sw], lambda e, sm=sm, rw=rw: e.reciprocal(
                        scl[:, rw:rw + 1], scl[:, sm:sm + 1]))
                    tok_scg[(w, ci)] = dve.op(
                        [t_sw, t_sxc, tok_epi.get((w, NTB - 1, ci - NPB))],
                        lambda e, sm=sm, cg=cg: e.tensor_mul(
                            scl[:, cg:cg + 1], scl[:, sm:sm + 1], scl[:, 5:6]))
                    t_i = dve.op([t_rw, state["i32rd"]],
                                 lambda e, b=b, ws_=ws_, rw=rw: e.tensor_scalar(
                                     i32scr[:, 0:D], ws_[b][:, :],
                                     scl[:, rw:rw + 1], None, ALU.mult))
                    tok_wread[(w, ci)] = t_i
                    t_k = dve.op([t_i, tok_tp.get((w, ci - NPB))],
                                 lambda e, b=b, ktm_=ktm_: e.tensor_copy(
                                     ktm_[b][:, :], i32scr[:, 0:D]))
                    state["i32rd"] = t_k
                    tok_k[(w, ci)] = t_k
                    tok_tp[(w, ci)] = act.dma(
                        [t_k] + [tok_epi.get((w, tb, ci - NPB)) for tb in range(NTB)],
                        lambda e, b=b, ktm_=ktm_, kwT_=kwT_: e.dma_start_transpose(
                            kwT_[b].rr("p (a b) -> p a b", a=DC), ktm_[b][:, :]))

            for pp in range(NPB):
                prep_b(pp)
            for ci in range(CI):
                b = ci % NPB
                b2 = ci % 2
                for half in range(2):
                    tbs = (2 * half, 2 * half + 1)
                    pe.waits([tok_tp[("g", ci)], tok_tp[("u", ci)], tok_c3_sil])
                    pe.waits([tok_epi.get(("g", tb, ci - 1)) for tb in tbs] +
                             [tok_epi.get(("u", tb, ci - 1)) for tb in tbs] +
                             [tok_epi.get(("p", tb, ci - 1)) for tb in tbs])
                    tok_mm = None
                    for hc in range(DC):
                        for w, kwT_, ps in [("g", kwT_g, ps_g), ("u", kwT_u, ps_u)]:
                            for tb in tbs:
                                lastm = (hc == DC - 1 and w == "u" and tb == tbs[-1])
                                tok_mm = pe.op([], lambda e, b=b, hc=hc, kwT_=kwT_, ps=ps, tb=tb: e.matmul(
                                    ps[:, tb * TBW:(tb + 1) * TBW],
                                    kwT_[b][:, hc * 128:(hc + 1) * 128],
                                    kx[hc][:, tb * TBW:(tb + 1) * TBW],
                                    start=(hc == 0), stop=(hc == DC - 1)),
                                    inc=1 if lastm else 0)
                    for tb in tbs:
                        eb = tb % 2
                        war_sg = (tok_epi.get(("p", tb - 2, ci)) or
                                  tok_epi.get(("p", tb + 2, ci - 1)))
                        t_sg = act.op([tok_mm, tok_scg[("g", ci)], t_zc, war_sg],
                                      lambda e, b=b, tb=tb, eb=eb: e.activation(
                                          sg_scr[eb][:, :],
                                          ps_g[:, tb * TBW:(tb + 1) * TBW],
                                          AF.Silu, bias=zero_col[:, 0:1],
                                          scale=scl[:, 38 + b:39 + b]))
                        t_up = act.op([tok_mm, tok_scg[("u", ci)], war_sg],
                                      lambda e, b=b, tb=tb, eb=eb: e.activation(
                                          up_scr[eb][:, :],
                                          ps_u[:, tb * TBW:(tb + 1) * TBW],
                                          AF.Copy, bias=0.0,
                                          scale=scl[:, 41 + b:42 + b]))
                        tok_epi[("g", tb, ci)] = t_sg
                        tok_epi[("u", tb, ci)] = t_up
                        tok_epi[("p", tb, ci)] = dve.op(
                            [t_sg, t_up, tok_hdma[b2], tok_hread[b2]],
                            lambda e, b2=b2, tb=tb, eb=eb: e.tensor_mul(
                                h_tile[b2][:, tb * TBW:(tb + 1) * TBW],
                                sg_scr[eb][:, :], up_scr[eb][:, :]))
                t_hfull = [tok_epi[("p", tb, ci)] for tb in range(NTB)]
                rm = dve.op(t_hfull, lambda e, b2=b2: e.tensor_reduce(
                    scl[:, 2 + b2:3 + b2], h_tile[b2][:, :], AXX, ALU.max,
                    apply_absolute_value=True))
                if ci == 0:
                    t_hm = dve.op([rm], lambda e: e.tensor_copy(scl[:, 0:1], scl[:, 2:3]))
                    hslot = 0
                else:
                    ns = 1 - hslot
                    t_hm = dve.op([rm, t_hm],
                                  lambda e, b2=b2, g=hslot, ns=ns: e.tensor_tensor(
                                      scl[:, ns:ns + 1], scl[:, g:g + 1],
                                      scl[:, 2 + b2:3 + b2], ALU.max))
                    hslot = ns
                tok_hread[b2] = rm
                if ci < CI - 2:
                    tok_hdma[b2] = pool.dma(t_hfull, lambda e, ci=ci, b2=b2: e.dma_start(
                        out=h_dram[ci * 128:(ci + 1) * 128, :], in_=h_tile[b2][:, :]))
                if ci + NPB < CI:
                    prep_b(ci + NPB)
        for nm in ([f"wsg{i}" for i in range(3)] + [f"wsu{i}" for i in range(3)] +
                   [f"ktmg{i}" for i in range(3)] + [f"ktmu{i}" for i in range(3)] +
                   [f"kwTg{i}" for i in range(3)] + [f"kwTu{i}" for i in range(3)] +
                   ["sgscr0", "sgscr1", "upscr0", "upscr1",
                    "i32scr"] + [f"kx{d}" for d in range(DC)]):
            ar.release(nm)

        softbar("B2")
        s_sb = [ar.alloc(f"ssb{d}", T, BF16) for d in range(DC)]
        kwdT = [ar.alloc(f"kwdT{j}", CI * 128, BF16) for j in range(DC)]
        wd_stage = [ar.alloc(f"wds{i}", II, F32) for i in range(2)]
        ktm_wd = [ar.alloc(f"ktmwd{i}", II, BF16) for i in range(2)]
        i32b2 = ar.alloc("i32b2", 1408, I32)
        wd32m_a = ar.alloc("wd32m_a", 1152, F32)
        wd32m_b = ar.alloc("wd32m_b", 256, F32)

        t_b1 = pool.dma([t_hm], lambda e, g=hslot: e.dma_start(
            out=bh_dram, in_=scl[:, g:g + 1]))
        t_b2 = pool.dma([t_b1], lambda e: e.dma_start(
            out=gm_row[:, :], in_=bh_dram.rearrange("a b -> b a")))
        t_gh = dve.op([t_b2], lambda e: e.tensor_reduce(
            g11[:, 0:1], gm_row[:, :], AXX, ALU.max))
        t_ch1 = pool.dma([t_gh], lambda e: e.dma_start(out=cch_in, in_=g11[:, 0:1]))
        t_cc2 = pool.dma([t_ch1], lambda e: e.collective_compute(
            "AllReduce", ALU.max, replica_groups=[list(range(NCORES))],
            ins=[cch_in.opt()], outs=[cch_out.opt()]), inc=1)
        t_ch2 = pool.dma([t_cc2], lambda e: e.dma_start(
            out=g11[:, 0:1], in_=cch_out))

        if True:
            s_ps = psumA[:, 0:2 * TBW]
            tok_b_act = (act.sem, act.n)
            t_sev = {}
            seq = 0
            for ds in range(DC):
                for tb in range(NTB):
                    half = seq % 2
                    pe.waits([t_sev.get(seq - 2), tok_b_act])
                    tok_s = None
                    for d in range(DC):
                        tok_s = pe.op([t_wload[("wo", d)], t_yg[d]],
                                      lambda e, ds=ds, d=d, tb=tb, half=half: e.matmul(
                                          s_ps[:, half * TBW:(half + 1) * TBW],
                                          wo_sb[d][:, ds * 128:(ds + 1) * 128],
                                          yb[d][:, tb * TBW:(tb + 1) * TBW],
                                          start=(d == 0), stop=(d == DC - 1)),
                                      inc=1 if d == DC - 1 else 0)
                    t_sev[seq] = act.op([tok_s], lambda e, ds=ds, tb=tb, half=half: e.copy(
                        s_sb[ds][:, tb * TBW:(tb + 1) * TBW],
                        s_ps[:, half * TBW:(half + 1) * TBW]))
                    seq += 1

            t_wdq = {}
            t_wdi = {}
            t_wdtp = {}
            t_swd = {}
            prev_i = None
            prev_a1 = None
            t_a1_war = {}
            for jc in range(DC):
                b = jc % 2
                tl = syn.dma([t_wdq.get(jc - 2), t_wdi.get(jc - 2),
                              t_a1_war.get(jc - 2)],
                             lambda e, jc=jc, b=b: e.dma_start(
                    out=wd_stage[b][:, :], in_=wd_s[jc * 128:(jc + 1) * 128, :]))
                t_red = dve.op([tl], lambda e, b=b: e.tensor_reduce(
                    scl[:, 2:3], wd_stage[b][:, :], AXX, ALU.max,
                    apply_absolute_value=True))
                t_swd[jc] = dve.op([t_red], lambda e, jc=jc: e.tensor_scalar(
                    scl[:, 14 + jc:15 + jc], scl[:, 2:3], R127, None, ALU.mult))
                rws = 8
                t_rw = dve.op([t_swd[jc]],
                              lambda e, jc=jc, rws=rws: e.reciprocal(
                                  scl[:, rws:rws + 1], scl[:, 14 + jc:15 + jc]))
                kb2 = jc % 2
                t_i = dve.op([t_rw, prev_i],
                             lambda e, b=b: e.tensor_scalar(
                                 i32b2[:, :],
                                 wd_stage[b][:, 0:1408],
                                 scl[:, rws:rws + 1], None, ALU.mult))
                t_k0 = dve.op([t_i, t_wdtp.get(jc - 2)],
                              lambda e, kb2=kb2: e.tensor_copy(
                                  ktm_wd[kb2][:, 0:1408], i32b2[:, :]))
                prev_i = t_k0
                t_wdi[jc] = t_i
                t_i1 = dve.op([t_k0],
                              lambda e, b=b, rws=rws: e.tensor_scalar(
                                  i32b2[:, :],
                                  wd_stage[b][:, 1408:2816],
                                  scl[:, rws:rws + 1], None, ALU.mult))
                t_k1 = dve.op([t_i1, t_wdtp.get(jc - 2)],
                              lambda e, kb2=kb2: e.tensor_copy(
                                  ktm_wd[kb2][:, 1408:2816], i32b2[:, :]))
                prev_i = t_k1
                t_a1_war[jc] = t_k1
                tk_half = [t_k0, t_k1]
                t_wdq[jc] = tk_half[-1]
                tpq = act if jc % 2 == 0 else syn
                t_wdtp[jc] = tpq.dma(tk_half, lambda e, jc=jc, kb2=kb2: e.dma_start_transpose(
                    kwdT[jc].rr("p (a b) -> p a b", a=CI), ktm_wd[kb2][:, :]))

            t_grow2 = dve.op([t_ch2], lambda e: e.tensor_single_scalar(
                gm_row[:, :], ones_f32r[:, :], g11[0:1, 0:1], ALU.mult))
            t_b3h = pool.dma([t_grow2], lambda e: e.dma_start(
                out=bh_dram.rearrange("a b -> b a"), in_=gm_row[:, :]))
            t_b4h = pool.dma([t_b3h], lambda e: e.dma_start(
                out=scl[:, 6:7], in_=bh_dram))
            t_t1h = dve.op([t_b4h, t_sig], lambda e: e.tensor_mul(
                scl[:, 7:8], scl[:, 6:7], sig_sb[:, 1:2]))
            t_shc = dve.op([t_t1h], lambda e: e.tensor_scalar(
                scl[:, 31:32], scl[:, 7:8], R127, None, ALU.mult))
            t_rhc = dve.op([t_shc], lambda e: e.reciprocal(scl[:, 30:31], scl[:, 31:32]))
            t_s2 = {}
            for jc in range(DC):
                t_s2[jc] = dve.op([t_swd[jc], t_shc], lambda e, jc=jc: e.tensor_mul(
                    scl[:, 22 + jc:23 + jc], scl[:, 14 + jc:15 + jc],
                    scl[:, 31:32]))
        for nm in (["wds0", "wds1", "ktmwd0", "ktmwd1", "i32b2", "wd32m_a", "wd32m_b"] +
                   [f"wo{d}" for d in range(DC)] + [f"yb{d}" for d in range(DC)]):
            ar.release(nm)

        softbar("D")
        khtb = [ar.alloc(f"khtb{i}", CI * TBW, BF16) for i in range(2)]
        h_stage = [ar.alloc(f"hstg{i}", TBW, F32) for i in range(6)]
        ud_scr = [ar.alloc(f"udscr{i}", TBW, F32) for i in range(3)]
        i32d = [ar.alloc(f"i32d{i}", TBW, I32) for i in range(3)]
        g_scr = [ar.alloc(f"gscr{i}", TBW, BF16) for i in range(4)]
        sg_d = [ar.alloc(f"sgd{i}", TBW, F32) for i in range(4)]
        o_scr = [ar.alloc(f"oscr{i}", TBW, F32) for i in range(4)]
        xt_d = [ar.alloc(f"xtd{i}", TBW, F32) for i in range(4)]

        if True:
            dn_ps = [psumA[:, :], psumB[:, :]]
            tok_b2_act = (act.sem, act.n)
            tok_kh = {}
            tok_mmD = {}
            tok_oD = {}
            tok_gD = {}
            tok_sg_prev = {}
            tok_o_prev = {}
            tok_od_prev = {}

            ci_list = [CI - 2, CI - 1] + list(range(CI - 2))

            def quant_d(tb):
                kb = tb % 2
                for idx, ci in enumerate(ci_list):
                    s3 = tb * CI + idx
                    hb = s3 % 6
                    ub = s3 % 3
                    if ci >= CI - 2:
                        t_u = act.op([t_rhc, t_zc, tok_kh.get(s3 - 3)],
                                     lambda e, ci=ci, tb=tb, ub=ub: e.activation(
                                         ud_scr[ub][:, :],
                                         h_tile[ci % 2][:, tb * TBW:(tb + 1) * TBW],
                                         AF.Copy, bias=0.0, scale=scl[:, 30:31]))
                    else:
                        t_hl = syn.dma([tok_kh.get(s3 - 6)],
                                       lambda e, tb=tb, ci=ci, hb=hb: e.dma_start(
                                           out=h_stage[hb][:, :],
                                           in_=h_dram[ci * 128:(ci + 1) * 128,
                                                      tb * TBW:(tb + 1) * TBW]))
                        t_u = act.op([t_hl, t_rhc, t_zc, tok_kh.get(s3 - 3)],
                                     lambda e, hb=hb, ub=ub: e.activation(
                                         ud_scr[ub][:, :], h_stage[hb][:, :], AF.Copy,
                                         bias=0.0, scale=scl[:, 30:31]))
                    qeng = dve
                    t_i = qeng.op([t_u, tok_kh.get(s3 - 3)],
                                  lambda e, ub=ub: e.tensor_scalar(
                                      i32d[ub][:, :], ud_scr[ub][:, :],
                                      CLIP_LO, CLIP_HI, ALU.max, ALU.min))
                    tok_kh[s3] = qeng.op([t_i, tok_mmD.get((tb - 2, 3))],
                                         lambda e, kb=kb, ci=ci, ub=ub: e.tensor_copy(
                                             khtb[kb][:, ci * TBW:(ci + 1) * TBW],
                                             i32d[ub][:, :]))

            quant_d(0)
            for tb in range(NTB):
                kb = tb % 2
                if tb + 1 < NTB:
                    quant_d(tb + 1)

                for jq in range(4):
                    jcs = [2 * jq, 2 * jq + 1]
                    psrc = dn_ps[jq // 2]
                    pcol = (jq % 2) * 2 * TBW
                    pe.waits([tok_gD.get((tb - 1, jc)) for jc in jcs])
                    pe.waits([t_wdtp[jc] for jc in jcs] + [tok_b2_act])
                    tok_mm = None
                    for idx, ci in enumerate(ci_list):
                        pe.waits([tok_kh[tb * CI + idx]])
                        for jc in jcs:
                            lastm = (idx == CI - 1 and jc == jcs[-1])
                            tok_mm = pe.op([], lambda e, psrc=psrc, pcol=pcol, jc=jc,
                                           ci=ci, kb=kb, jq=jq, idx=idx: e.matmul(
                                psrc[:, pcol + (jc - 2 * jq) * TBW:
                                     pcol + (jc - 2 * jq + 1) * TBW],
                                kwdT[jc][:, ci * 128:(ci + 1) * 128],
                                khtb[kb][:, ci * TBW:(ci + 1) * TBW],
                                start=(idx == 0), stop=(idx == CI - 1)),
                                inc=1 if lastm else 0)
                    tok_mmD[(tb, jq)] = tok_mm
                    for jc in jcs:
                        eb = jc % 4
                        t_xl = pool.dma([tok_o_prev.get(eb)],
                                        lambda e, tb=tb, jc=jc, eb=eb: e.dma_start(
                                            out=xt_d[eb][:, :],
                                            in_=xT[jc * 128:(jc + 1) * 128,
                                                   tb * TBW:(tb + 1) * TBW]))
                        t_g = act.op([tok_mm, tok_sg_prev.get(eb), t_zc,
                                      t_s2[jc]],
                                     lambda e, psrc=psrc, pcol=pcol, jq=jq, jc=jc,
                                     eb=eb: e.activation(
                                         g_scr[eb][:, :],
                                         psrc[:, pcol + (jc - 2 * jq) * TBW:
                                              pcol + (jc - 2 * jq + 1) * TBW],
                                         AF.Silu, bias=zero_col[:, 0:1],
                                         scale=scl[:, 22 + jc:23 + jc]))
                        tok_gD[(tb, jc)] = t_g
                        t_sg = dve.op([t_g, tok_o_prev.get(eb)],
                                      lambda e, jc=jc, tb=tb, eb=eb: e.tensor_mul(
                                          sg_d[eb][:, :], g_scr[eb][:, :],
                                          s_sb[jc][:, tb * TBW:(tb + 1) * TBW]))
                        tok_sg_prev[eb] = t_sg
                        o_eng = dve if tb == NTB - 1 else pool
                        t_o = o_eng.op([t_sg, t_xl, tok_od_prev.get(eb)],
                                       lambda e, eb=eb: e.tensor_add(
                                           o_scr[eb][:, :], sg_d[eb][:, :],
                                           xt_d[eb][:, :]))
                        tok_o_prev[eb] = t_o
                        odq = act if jc % 2 == 0 else syn
                        t_od = odq.dma([t_o], lambda e, tb=tb, jc=jc, eb=eb: e.dma_start(
                            out=outT[jc * 128:(jc + 1) * 128, tb * TBW:(tb + 1) * TBW],
                            in_=o_scr[eb][:, :]))
                        tok_oD[(tb, jc)] = t_od
                        tok_od_prev[eb] = t_od

            syn.waits([tok_oD[(tb, jc)] for tb in range(NTB) for jc in range(DC)])
            flush("ALL")

    return nc


_NC_CACHE = None


def kernel(L0_attn_x, L0_attn_Wq, L0_attn_Wv, L0_attn_Wg, L0_attn_Wo,
           L0_attn_beta, L0_attn_ln_gamma, L0_attn_ln_beta,
           L0_ffn_clip_x, L0_ffn_clip_down,
           L0_ffn_w_gate, L0_ffn_w_up, L0_ffn_w_down):
    global _NC_CACHE
    x = np.asarray(L0_attn_x, np.float32)
    wq = np.asarray(L0_attn_Wq, np.float32)
    wv = np.asarray(L0_attn_Wv, np.float32)
    wg = np.asarray(L0_attn_Wg, np.float32)
    wo = np.asarray(L0_attn_Wo, np.float32)
    beta = np.asarray(L0_attn_beta, np.float32)
    gam = np.asarray(L0_attn_ln_gamma, np.float32)
    bln = np.asarray(L0_attn_ln_beta, np.float32)
    cx = np.asarray(L0_ffn_clip_x, np.float32)
    cd = np.asarray(L0_ffn_clip_down, np.float32)
    wgate = np.ascontiguousarray(np.asarray(L0_ffn_w_gate, np.float32))
    wup = np.ascontiguousarray(np.asarray(L0_ffn_w_up, np.float32))
    wdown = np.ascontiguousarray(np.asarray(L0_ffn_w_down, np.float32))

    wqT = np.ascontiguousarray(wq.T)
    wvT = np.ascontiguousarray(wv.T)
    wgT = np.ascontiguousarray(wg.T)
    woT = np.ascontiguousarray(wo.T)
    beta_pc = np.ascontiguousarray(np.repeat(beta.reshape(DC, 2), 64, axis=1).T)
    gam_pc = np.ascontiguousarray(gam.reshape(DC, 128).T)
    bln_pc = np.ascontiguousarray(bln.reshape(DC, 128).T)
    clip2 = np.ascontiguousarray(np.repeat(
        np.concatenate([cx.reshape(1), cd.reshape(1)]).reshape(1, 2), 128, axis=0))

    if _NC_CACHE is None:
        _NC_CACHE = build_nc()
    nc = _NC_CACHE

    in_maps = []
    for b in range(NCORES):
        in_maps.append({
            "xT": np.ascontiguousarray(x[b].T),
            "wqT": wqT, "wvT": wvT, "wgT": wgT, "woT": woT,
            "wg_s": wgate, "wu_s": wup, "wd_s": wdown,
            "beta_pc": beta_pc, "gam_pc": gam_pc, "bln_pc": bln_pc,
            "clip2": clip2,
        })
    res = run_bass_kernel_spmd(nc, in_maps, core_ids=list(range(NCORES)))
    out = np.stack([np.asarray(res.results[b]["outT"]).T for b in range(NCORES)], axis=0)
    return np.ascontiguousarray(out.astype(np.float32))
